# version 09
# baseline (speedup 1.0000x reference)
import functools
import numpy as np
import ml_dtypes
from contextlib import ExitStack

import concourse.bass as bass
import concourse.mybir as mybir
from concourse.bass_utils import run_bass_kernel_spmd

F32 = mybir.dt.float32
BF16 = mybir.dt.bfloat16
I8 = mybir.dt.int8
U32 = mybir.dt.uint32
AF = mybir.ActivationFunctionType
ALU = mybir.AluOpType
AX = mybir.AxisListType.X

L = 4096
KF = 2049
KC = 17
HT = 32
ITI = 86
INTER = 11008
EPS = 1e-5
TOPK = 8
NROLL = 1
NCORES = 8


@functools.lru_cache(maxsize=1)
def host_tables():
    bf = ml_dtypes.bfloat16
    t = np.arange(L, dtype=np.float64)
    k = np.arange(KC * 128, dtype=np.float64)
    ang = 2.0 * np.pi / L
    KT = np.outer(t, k)
    cosf = np.cos(ang * KT)
    sinf = np.sin(ang * KT)
    cosf[:, KF:] = 0.0
    sinf[:, KF:] = 0.0
    cos_t = np.ascontiguousarray(cosf.reshape(HT, 128, KC, 128).transpose(2, 0, 1, 3).astype(bf))
    sin_t = np.ascontiguousarray(sinf.reshape(HT, 128, KC, 128).transpose(2, 0, 1, 3).astype(bf))
    ck = np.full(KC * 128, 2.0); ck[0] = 1.0; ck[2048] = 1.0; ck[KF:] = 0.0
    icos = (ck[:, None] * np.cos(ang * np.outer(k, t))) / L
    icos[KF:, :] = 0.0
    icos_t = np.ascontiguousarray(icos.reshape(KC, 128, L).astype(bf))
    ident = np.eye(128, dtype=bf)
    return cos_t, sin_t, icos_t, ident


class BK:

    ENGS = ("v", "s", "t", "g", "y")

    def __init__(self, nc, st):
        self.nc = nc
        self.streams = {e: [] for e in self.ENGS}
        self.sems = {}
        self.lastw = {}
        self.buf = {}
        self.st = st
        for e in self.ENGS:
            self._mksem("S" + e)
        self.dma_pool = {}
        self.dma_rr = {}
        for e in ("y", "g", "s"):
            names = []
            for i in range(8):
                n = f"D{e}{i}"
                self._mksem(n)
                names.append(n)
            self.dma_pool[e] = names
            self.dma_rr[e] = 0

    def _mksem(self, name):
        h = self.st.enter_context(self.nc.semaphore(name))
        self.sems[name] = [h, 0]

    def _wait(self, e, tok):
        if tok is None:
            return
        name, cnt = tok
        if self.lastw.get((e, name), 0) >= cnt:
            return
        self.lastw[(e, name)] = cnt
        h = self.sems[name][0]
        self.streams[e].append(lambda eng, h=h, cnt=cnt: eng.wait_ge(h, cnt))

    def op(self, e, build, reads=(), writes=(), deps=(), dma=False):
        toks = list(deps)
        for b in reads:
            s = self.buf.get(b)
            if s and s["w"]:
                toks.append(s["w"])
        for b in writes:
            s = self.buf.get(b)
            if s:
                if s["w"]:
                    toks.append(s["w"])
                toks.extend(s["r"])
        for tok in toks:
            self._wait(e, tok)
        if dma:
            pool = self.dma_pool[e]
            name = pool[self.dma_rr[e] % len(pool)]
            self.dma_rr[e] += 1
            srec = self.sems[name]
            self._wait(e, (name, srec[1]))
            srec[1] += 16
        else:
            name = "S" + e
            srec = self.sems[name]
            srec[1] += 1
        cnt = srec[1]
        h = srec[0]
        inc = 16 if dma else 1
        self.streams[e].append(
            lambda eng, build=build, h=h, inc=inc: build(eng).then_inc(h, inc))
        tok = (name, cnt)
        for b in reads:
            self.buf.setdefault(b, {"w": None, "r": []})["r"].append(tok)
        for b in writes:
            self.buf[b] = {"w": tok, "r": []}
        return tok

    def barrier(self):
        toks = [(n, s[1]) for n, s in self.sems.items() if s[1] > 0]
        for e in self.ENGS:
            for tok in toks:
                self._wait(e, tok)
        for b in self.buf.values():
            b["r"] = []

    def replay(self, block):
        for e, meth in (("v", block.vector), ("s", block.scalar), ("t", block.tensor),
                        ("g", block.gpsimd), ("y", block.sync)):
            stream = self.streams[e]
            if stream:
                meth(lambda eng, stream=stream: [f(eng) for f in stream])


def emit_ln_bn(k, stats, bnb, src_key, src_ap, sx=""):
    m = stats[:, 0:1]; rstd = stats[:, 1:2]; tmp = stats[:, 5:6]
    for c in range(8):
        k.op("v", lambda v, c=c: v.bn_stats(bnb[:, c * 6:(c + 1) * 6],
                                            src_ap[:, c * 512:(c + 1) * 512]),
             reads=(src_key,), writes=("bnb" + sx,))
    k.op("v", lambda v: v.bn_aggr(stats[:, 0:2], bnb[:]), reads=("bnb" + sx,),
         writes=("stm" + sx, "stvar" + sx))
    k.op("s", lambda s: s.activation(out=tmp, in_=rstd, func=AF.Ln, bias=EPS),
         reads=("stvar" + sx, "cst"), writes=("sttmp" + sx,))
    k.op("s", lambda s: s.activation(out=rstd, in_=tmp, func=AF.Exp, scale=-0.5),
         reads=("sttmp" + sx, "cst"), writes=("strstd" + sx, "stvar" + sx))
    return m, rstd


def emit_rstd_from(k, stats, s1ap, s2ap, n):
    m = stats[:, 0:1]; rstd = stats[:, 1:2]
    msq = stats[:, 4:5]; var = stats[:, 5:6]
    k.op("v", lambda v: v.tensor_scalar_mul(m, s1ap, 1.0 / n), reads=("lnacc",), writes=("stm",))
    k.op("v", lambda v: v.tensor_mul(msq, m, m), reads=("stm",), writes=("stmsq",))
    k.op("v", lambda v: v.scalar_tensor_tensor(out=var, in0=s2ap, scalar=1.0 / n, in1=msq,
                                               op0=ALU.mult, op1=ALU.subtract),
         reads=("lnacc", "stmsq"), writes=("stvar",))
    k.op("s", lambda s: s.activation(out=var, in_=var, func=AF.Ln, bias=EPS),
         reads=("stvar", "cst"), writes=("stvar",))
    k.op("s", lambda s: s.activation(out=rstd, in_=var, func=AF.Exp, scale=-0.5),
         reads=("stvar", "cst"), writes=("strstd",))
    return m, rstd


def build_nc(NT=8):
    T = NT * 128
    RNC = 2 if NT >= 2 else 1
    NPASS = NT // RNC
    NRC = (T + 511) // 512
    HIDT = min(512, T)
    NHB = T // HIDT
    NTT = HIDT // 128
    PW = max(512, 512 * NRC)

    nc = bass.Bass("TRN2", target_bir_lowering=False)
    x_d = nc.dram_tensor("x", [T, L], F32, kind="ExternalInput")
    wg_d = nc.dram_tensor("wg", [ITI, 128, HT, 128], I8, kind="ExternalInput")
    wu_d = nc.dram_tensor("wu", [ITI, 128, HT, 128], I8, kind="ExternalInput")
    wd_d = nc.dram_tensor("wd", [ITI, 128, L], I8, kind="ExternalInput")
    sg_d = nc.dram_tensor("sg", [128, ITI], F32, kind="ExternalInput")
    su_d = nc.dram_tensor("su", [128, ITI], F32, kind="ExternalInput")
    sdb_d = nc.dram_tensor("sdb", [128, L], F32, kind="ExternalInput")
    cos_d = nc.dram_tensor("cos_t", [KC, HT, 128, 128], BF16, kind="ExternalInput")
    sin_d = nc.dram_tensor("sin_t", [KC, HT, 128, 128], BF16, kind="ExternalInput")
    icos_d = nc.dram_tensor("icos_t", [KC, 128, L], BF16, kind="ExternalInput")
    id_d = nc.dram_tensor("ident", [128, 128], BF16, kind="ExternalInput")
    y_d = nc.dram_tensor("y", [T, L], F32, kind="ExternalOutput")

    a2_d = nc.dram_tensor("a2_scr", [T, 2 * L], F32, kind="Internal")
    a2b_d = nc.dram_tensor("a2b_scr", [T, 2 * L], BF16, kind="Internal")
    r_d = nc.dram_tensor("r_scr", [T, L], F32, kind="Internal")
    dn_d = nc.dram_tensor("dn_scr", [T, L], F32, kind="Internal")
    atf_d = nc.dram_tensor("atf_scr", [HT, 128, T], BF16, kind="Internal")
    hb_d = nc.dram_tensor("hb_scr", [ITI, 128, HIDT], BF16, kind="Internal") if NHB > 1 else None

    with ExitStack() as st:
        k = BK(nc, st)
        ident = st.enter_context(nc.sbuf_tensor("identb", [128, 128], BF16))
        stats2 = [st.enter_context(nc.sbuf_tensor(f"stats{q}", [128, 8], F32)) for q in range(2)]
        stats = stats2[0]
        bnb2 = [st.enter_context(nc.sbuf_tensor(f"bnb{q}", [128, 6 * 8], F32)) for q in range(2)]
        sg_sb = st.enter_context(nc.sbuf_tensor("sg_sb", [128, ITI], F32))
        su_sb = st.enter_context(nc.sbuf_tensor("su_sb", [128, ITI], F32))
        top8 = [st.enter_context(nc.sbuf_tensor(f"top8{q}", [128, 8], BF16)) for q in range(2)]
        t8f = [st.enter_context(nc.sbuf_tensor(f"t8f{q}", [128, 8], F32)) for q in range(2)]
        idx8 = [st.enter_context(nc.sbuf_tensor(f"idx8{q}", [128, 8], U32)) for q in range(2)]
        smb = [st.enter_context(nc.sbuf_tensor(f"smb{q}", [128, 8], F32)) for q in range(2)]
        w8 = [st.enter_context(nc.sbuf_tensor(f"w8b{q}", [128, 8], F32)) for q in range(2)]
        ssum = [st.enter_context(nc.sbuf_tensor(f"ssumb{q}", [128, 2], F32)) for q in range(2)]
        iot = [st.enter_context(nc.sbuf_tensor(f"iotb{q}", [128, 8], U32)) for q in range(2)]
        gidx = [st.enter_context(nc.sbuf_tensor(f"gidxb{q}", [128, 8], U32)) for q in range(2)]
        lnacc = st.enter_context(nc.sbuf_tensor("lnacc", [128, 3 * NT], F32))
        cst = st.enter_context(nc.sbuf_tensor("cstb", [128, 4], F32))
        for ci, cv in enumerate((EPS, -0.5, 1.0, 0.0)):
            nc.const_aps.register(F32, cv, cst[:, ci:ci + 1])
        k.op("v", lambda v: v.memset(cst[:, 0:1], EPS), writes=("cst",))
        k.op("v", lambda v: v.memset(cst[:, 1:2], -0.5), reads=("cst",), writes=("cst",))
        k.op("v", lambda v: v.memset(cst[:, 2:3], 1.0), reads=("cst",), writes=("cst",))
        k.op("v", lambda v: v.memset(cst[:, 3:4], 0.0), reads=("cst",), writes=("cst",))
        k.barrier()

        k.op("y", lambda y: y.dma_start(out=ident[:], in_=id_d[:]), writes=("ident",), dma=True)
        k.op("y", lambda y: y.dma_start(out=sg_sb[:], in_=sg_d[:]), writes=("sg",), dma=True)
        k.op("y", lambda y: y.dma_start(out=su_sb[:], in_=su_d[:]), writes=("su",), dma=True)

        s_aT = ExitStack()
        aT = s_aT.enter_context(nc.sbuf_tensor("aT", [128, HT, T], BF16))
        s_pT = ExitStack()
        pT = s_pT.enter_context(nc.sbuf_tensor("pT", [128, KC, T], BF16, side="right"))
        s_a1 = ExitStack()
        xs = [s_a1.enter_context(nc.sbuf_tensor(f"xs{p}", [128, L], F32)) for p in range(2)]
        abf = [s_a1.enter_context(nc.sbuf_tensor(f"abf{p}", [128, L], BF16)) for p in range(2)]
        sq = s_a1.enter_context(nc.sbuf_tensor("sq", [128, 2048], F32))
        cosb = [s_a1.enter_context(nc.sbuf_tensor(f"cosb{p}", [128, HT, 128], BF16)) for p in range(2)]
        sinb = [s_a1.enter_context(nc.sbuf_tensor(f"sinb{p}", [128, HT, 128], BF16)) for p in range(2)]
        tp_ps = [s_a1.enter_context(nc.psum_tensor(f"tp{p}", [128, 1024], BF16)) for p in range(2)]
        re_ps = [s_a1.enter_context(nc.psum_tensor(f"re{p}", [128, 512], F32)) for p in range(2)]
        im_ps = [s_a1.enter_context(nc.psum_tensor(f"im{p}", [128, 512], F32)) for p in range(2)]
        HALF = (NT + 1) // 2

        def emit_a1(i):
            p = i % 2
            rows = slice(i * 128, (i + 1) * 128)
            k.op("y", lambda y, rows=rows, p=p: y.dma_start(out=xs[p][:], in_=x_d[rows, :]),
                 writes=(f"xs{p}",), dma=True)
            m, rstd = emit_ln_bn(k, stats2[p], bnb2[p], f"xs{p}", xs[p][:], sx=str(p))
            k.op("v", lambda v, p=p, m=m, rstd=rstd: v.tensor_scalar(
                    out=xs[p][:], in0=xs[p][:], scalar1=m, scalar2=rstd,
                    op0=ALU.subtract, op1=ALU.mult),
                 reads=(f"xs{p}", f"stm{p}", f"strstd{p}"), writes=(f"xs{p}",))
            k.op("y", lambda y, rows=rows, p=p: y.dma_start(out=a2_d[rows, 0:L], in_=xs[p][:]),
                 reads=(f"xs{p}",), writes=("a2w",), dma=True)
            k.op("y", lambda y, rows=rows, p=p: y.dma_start(out=a2_d[rows, L:2 * L], in_=xs[p][:]),
                 reads=(f"xs{p}",), writes=("a2w",), dma=True)
            k.op("v", lambda v, p=p: v.tensor_copy(out=abf[p][:], in_=xs[p][:]),
                 reads=(f"xs{p}",), writes=(f"abf{p}",))
            k.op("g", lambda g, rows=rows, p=p: g.dma_start(out=a2b_d[rows, 0:L], in_=abf[p][:]),
                 reads=(f"abf{p}",), writes=("a2bw",), dma=True)
            k.op("g", lambda g, rows=rows, p=p: g.dma_start(out=a2b_d[rows, L:2 * L], in_=abf[p][:]),
                 reads=(f"abf{p}",), writes=("a2bw",), dma=True)

        def emit_tp(i, akey):
            p = i % 2
            for tc in range(HT):
                pp = tc % 2
                k.op("t", lambda t, tc=tc, pp=pp, p=p: t.transpose(
                        tp_ps[pp][:, 0:128], abf[p][:, tc * 128:(tc + 1) * 128], ident[:]),
                     reads=(f"abf{p}", "ident"), writes=(f"tp{pp}",))
                k.op("s", lambda sc, tc=tc, pp=pp, i=i: sc.copy(
                        out=aT[:, tc, i * 128:(i + 1) * 128], in_=tp_ps[pp][:, 0:128]),
                     reads=(f"tp{pp}",), writes=(akey,))

        def load_tab(kc):
            p = kc % 2
            k.op("s", lambda sc, kc=kc, p=p: sc.dma_start(
                    out=cosb[p][:], in_=cos_d[kc].rearrange("a p c -> p a c")),
                 writes=(f"cosb{p}",), dma=True)
            k.op("s", lambda sc, kc=kc, p=p: sc.dma_start(
                    out=sinb[p][:], in_=sin_d[kc].rearrange("a p c -> p a c")),
                 writes=(f"sinb{p}",), dma=True)

        def emit_fwd(rc, akey):
            w = min(512, T - rc * 512)
            if rc > 0:
                load_tab(0)
            for kc in range(KC):
                p = kc % 2
                if kc + 1 < KC:
                    load_tab(kc + 1)

                def fwd(t, p=p, rc=rc, w=w):
                    last = None
                    for tc in range(HT):
                        st_ = (tc == 0)
                        sp = (tc == HT - 1)
                        t.matmul(re_ps[p][:, 0:w], cosb[p][:, tc, :],
                                 aT[:, tc, rc * 512:rc * 512 + w], start=st_, stop=sp)
                        last = t.matmul(im_ps[p][:, 0:w], sinb[p][:, tc, :],
                                        aT[:, tc, rc * 512:rc * 512 + w], start=st_, stop=sp)
                    return last
                k.op("t", fwd, reads=(f"cosb{p}", f"sinb{p}", akey), writes=(f"re{p}", f"im{p}"))
                k.op("s", lambda sc, p=p, w=w: sc.activation(
                        out=sq[:, 0:w], in_=re_ps[p][:, 0:w], func=AF.Square),
                     reads=(f"re{p}", "cst"), writes=("sqa",))
                k.op("s", lambda sc, p=p, w=w: sc.activation(
                        out=sq[:, 1024:1024 + w], in_=im_ps[p][:, 0:w], func=AF.Square),
                     reads=(f"im{p}", "cst"), writes=("sqb",))
                k.op("v", lambda v, kc=kc, rc=rc, w=w: v.tensor_add(
                        pT[:, kc, rc * 512:rc * 512 + w], sq[:, 0:w], sq[:, 1024:1024 + w]),
                     reads=("sqa", "sqb"), writes=("pT",))

        load_tab(0)
        for i in range(NT):
            emit_a1(i)
            if i < HALF:
                emit_tp(i, "aTa")
        emit_fwd(0, "aTa")
        if NT > 4:
            for i in range(HALF, NT):
                p = i % 2
                rows = slice(i * 128, (i + 1) * 128)
                k.op("y", lambda y, rows=rows, p=p: y.dma_start(
                        out=abf[p][:], in_=a2b_d[rows, 0:L]),
                     reads=("a2bw",), writes=(f"abf{p}",), dma=True)
                emit_tp(i, "aTb")
            emit_fwd(1, "aTb")
        k.barrier()
        s_a1.close()
        s_aT.close()

        s_corr = ExitStack()
        corrF = s_corr.enter_context(nc.sbuf_tensor("corrF", [128, NT, L], BF16))
        s_att = ExitStack()
        HL = L // 4
        icosb = [s_att.enter_context(nc.sbuf_tensor(f"icosb{p}", [128, HL], BF16)) for p in range(3)]
        rollf = s_att.enter_context(nc.sbuf_tensor("rollf", [128, L], F32))
        rollb = ([s_att.enter_context(nc.sbuf_tensor(f"rollb{p}", [128, L], BF16))
                  for p in range(3)] if NROLL > 1 else [])
        da = [s_att.enter_context(nc.sbuf_tensor(f"da{q}", [128, L], F32)) for q in range(2)]
        stg = [s_att.enter_context(nc.sbuf_tensor(f"stg{p}", [128, HT, 128], BF16)) for p in range(2)]
        s_attp = ExitStack()
        cr_ps = [s_attp.enter_context(nc.psum_tensor(f"cr{r}", [128, HL], F32))
                 for r in range(2 * RNC)]

        def load_icos(n, seq):
            ps_, th, kc = seq[n]
            p = n % 3
            k.op("s", lambda s, kc=kc, th=th, p=p: s.dma_start(
                    out=icosb[p][:], in_=icos_d[kc, :, th * HL:(th + 1) * HL]),
                 writes=(f"icosb{p}",), dma=True)

        icos_seq = [(ps_, th, kc) for ps_ in range(NPASS) for th in range(4) for kc in range(KC)]
        load_icos(0, icos_seq)
        load_icos(1, icos_seq)
        nld = 2
        pend = []

        def emit_a456(i):
            q = i % 2
            sx = str(q)
            rows = slice(i * 128, (i + 1) * 128)
            k.op("v", lambda v, i=i, q=q: v.max(out=top8[q][:], in_=corrF[:, i, :]),
                 reads=(f"corr{i}",), writes=("top8" + sx,))
            k.op("v", lambda v, i=i, q=q: v.max_index(out=idx8[q][:], in_max=top8[q][:],
                                                 in_values=corrF[:, i, :]),
                 reads=("top8" + sx, f"corr{i}"), writes=("idx8" + sx,))
            k.op("v", lambda v, q=q: v.tensor_copy(out=t8f[q][:], in_=top8[q][:]),
                 reads=("top8" + sx,), writes=("t8f" + sx,))
            k.op("v", lambda v, q=q: v.tensor_scalar(out=smb[q][:], in0=t8f[q][:],
                                                scalar1=t8f[q][:, 0:1],
                                                scalar2=None, op0=ALU.subtract),
                 reads=("t8f" + sx,), writes=("smbb" + sx,))
            k.op("s", lambda s, q=q: s.activation(out=smb[q][:], in_=smb[q][:], func=AF.Exp),
                 reads=("smbb" + sx, "cst"), writes=("smbb" + sx,))
            k.op("v", lambda v, q=q: v.reduce_sum(ssum[q][:, 0:1], smb[q][:], axis=AX),
                 reads=("smbb" + sx,), writes=("ssum" + sx,))
            k.op("v", lambda v, q=q: v.reciprocal(ssum[q][:, 1:2], ssum[q][:, 0:1]),
                 reads=("ssum" + sx,), writes=("ssumr" + sx,))
            k.op("v", lambda v, q=q: v.tensor_scalar(out=w8[q][:], in0=smb[q][:],
                                                scalar1=ssum[q][:, 1:2],
                                                scalar2=None, op0=ALU.mult),
                 reads=("smbb" + sx, "ssumr" + sx), writes=("w8" + sx,))
            k.op("g", lambda g, i=i, q=q: g.iota(iot[q][:], pattern=[[0, 8]],
                                            base=i * 128 * 2 * L,
                                            channel_multiplier=2 * L),
                 writes=("iot" + sx,))
            k.op("v", lambda v, q=q: v.tensor_tensor(out=gidx[q][:], in0=idx8[q][:],
                                                in1=iot[q][:], op=ALU.add),
                 reads=("idx8" + sx, "iot" + sx), writes=("gidx" + sx,))
            k.op("g", lambda g, q=q: g.indirect_dma_start(
                    out=rollf[:], out_offset=None, in_=a2_d[:, :],
                    in_offset=bass.IndirectOffsetOnAxis(ap=gidx[q][:, 0:1], axis=1)),
                 reads=("gidx" + sx,), writes=("rollf",), dma=True)
            k.op("v", lambda v, q=q: v.tensor_scalar(
                    out=da[q][:], in0=rollf[:], scalar1=w8[q][:, 0:1], scalar2=None,
                    op0=ALU.mult),
                 reads=("rollf", "w8" + sx), writes=("da" + sx,))
            for j in range(1, NROLL):
                p = (i * (TOPK - 1) + j) % 3
                k.op("g", lambda g, j=j, p=p, q=q: g.indirect_dma_start(
                        out=rollb[p][:], out_offset=None, in_=a2b_d[:, :],
                        in_offset=bass.IndirectOffsetOnAxis(ap=gidx[q][:, j:j + 1], axis=1)),
                     reads=("gidx" + sx,), writes=(f"rollb{p}",), dma=True)
                k.op("v", lambda v, j=j, p=p, q=q: v.scalar_tensor_tensor(
                        out=da[q][:], in0=rollb[p][:], scalar=w8[q][:, j:j + 1], in1=da[q][:],
                        op0=ALU.mult, op1=ALU.add),
                     reads=(f"rollb{p}", "w8" + sx, "da" + sx), writes=("da" + sx,))
            m, rstd = emit_ln_bn(k, stats2[q], bnb2[q], "da" + sx, da[q][:], sx=sx)
            k.op("v", lambda v, q=q, m=m, rstd=rstd: v.tensor_scalar(
                    out=da[q][:], in0=da[q][:], scalar1=m, scalar2=rstd,
                    op0=ALU.subtract, op1=ALU.mult),
                 reads=("da" + sx, "stm" + sx, "strstd" + sx), writes=("da" + sx,))

        def emit_a456_B(i):
            q = i % 2
            sx = str(q)
            rows = slice(i * 128, (i + 1) * 128)
            k.op("g", lambda g, rows=rows, q=q: g.dma_start(
                    out=da[q][:], in_=x_d[rows, :], accum_op=ALU.add,
                    max_dma_last_dim=2048),
                 reads=("da" + sx,), writes=("da" + sx,), dma=True)
            k.op("y", lambda y, rows=rows, q=q: y.dma_start(out=r_d[rows, :], in_=da[q][:]),
                 reads=("da" + sx,), writes=("rw",), dma=True)
            m, rstd = emit_ln_bn(k, stats2[q], bnb2[q], "da" + sx, da[q][:], sx=sx)
            k.op("v", lambda v, i=i, q=q, m=m, rstd=rstd: v.tensor_scalar(
                    out=corrF[:, i, :], in0=da[q][:], scalar1=m, scalar2=rstd,
                    op0=ALU.subtract, op1=ALU.mult),
                 reads=("da" + sx, "stm" + sx, "strstd" + sx), writes=(f"corr{i}",))
        for ps_ in range(NPASS):
            for th in range(4):
                for kc in range(KC):
                    if nld < len(icos_seq):
                        load_icos(nld, icos_seq)
                        nld += 1
                    p = (ps_ * 4 * KC + th * KC + kc) % 3
                    cpar = ((ps_ * 4 + th) % 2) * RNC

                    def corr(t, ps_=ps_, kc=kc, p=p, cpar=cpar):
                        last = None
                        for rt in range(RNC):
                            row = (ps_ * RNC + rt) * 128
                            for tch in range(HL // 512):
                                last = t.matmul(cr_ps[cpar + rt][:, tch * 512:(tch + 1) * 512],
                                                pT[:, kc, row:row + 128],
                                                icosb[p][:, tch * 512:(tch + 1) * 512],
                                                start=(kc == 0), stop=(kc == KC - 1))
                        return last
                    k.op("t", corr, reads=(f"icosb{p}", "pT"),
                         writes=tuple(f"cr{cpar + rt}" for rt in range(RNC)))
                cpar = ((ps_ * 4 + th) % 2) * RNC
                for rt in range(RNC):
                    i = ps_ * RNC + rt
                    k.op("s", lambda sc, i=i, rt=rt, th=th, cpar=cpar: sc.copy(
                            out=corrF[:, i, th * HL:(th + 1) * HL], in_=cr_ps[cpar + rt][:]),
                         reads=(f"cr{cpar + rt}",), writes=(f"corr{i}",))
            for rt in range(RNC):
                i = ps_ * RNC + rt
                emit_a456(i)
                if pend:
                    emit_a456_B(pend.pop(0))
                pend.append(i)
        while pend:
            emit_a456_B(pend.pop(0))
        s_attp.close()
        s_tq = ExitStack()
        tq_ps = [s_tq.enter_context(nc.psum_tensor(f"tq{p}", [128, 1024], BF16)) for p in range(2)]
        for i in range(NT):
            sp2 = i % 2
            for tc in range(HT):
                pp = tc % 2
                k.op("t", lambda t, i=i, tc=tc, pp=pp: t.transpose(
                        tq_ps[pp][:, 0:128], corrF[:, i, tc * 128:(tc + 1) * 128], ident[:]),
                     reads=(f"corr{i}", "ident"), writes=(f"tq{pp}",))
                k.op("s", lambda s, tc=tc, pp=pp, sp2=sp2: s.copy(
                        out=stg[sp2][:, tc, :], in_=tq_ps[pp][:, 0:128]),
                     reads=(f"tq{pp}",), writes=(f"stg{sp2}",))
            for hf in range(2):
                k.op("y", lambda y, i=i, hf=hf, sp2=sp2: y.dma_start(
                        out=atf_d[hf * 16:(hf + 1) * 16, :, i * 128:(i + 1) * 128]
                            .rearrange("a p c -> p a c"),
                        in_=stg[sp2][:, hf * 16:(hf + 1) * 16, :]),
                     reads=(f"stg{sp2}",), writes=("atfw",), dma=True)
        k.barrier()
        s_tq.close()
        s_att.close()
        s_corr.close()
        s_pT.close()

        ffn = ExitStack()
        hid = ffn.enter_context(nc.sbuf_tensor("hid", [128, ITI, HIDT], BF16))
        s_gu = ExitStack()
        atfb = s_gu.enter_context(nc.sbuf_tensor("atfb", [128, HT, T], BF16))
        wgb = [s_gu.enter_context(nc.sbuf_tensor(f"wgb{h}", [128, 16, 128], BF16)) for h in range(2)]
        wub = [s_gu.enter_context(nc.sbuf_tensor(f"wub{h}", [128, 16, 128], BF16)) for h in range(2)]
        silu_s = [s_gu.enter_context(nc.sbuf_tensor(f"silu{p}", [128, T], F32)) for p in range(2)]
        hstg = [s_gu.enter_context(nc.sbuf_tensor(f"hstg{p}", [128, HIDT], BF16)) for p in range(2)]
        gps = [s_gu.enter_context(nc.psum_tensor(f"gps{q}", [128, PW], F32)) for q in range(4)]

        for hf in range(2):
            k.op("y", lambda y, hf=hf: y.dma_start(
                    out=atfb[:, hf * 16:(hf + 1) * 16, :],
                    in_=atf_d[hf * 16:(hf + 1) * 16, :, :].rearrange("a p c -> p a c")),
                 writes=("atfb",), dma=True)

        for ig in range(ITI):
            par = ig % 2
            pg = gps[par * 2 + 0]
            pu = gps[par * 2 + 1]
            pgk = f"gps{par * 2}"
            puk = f"gps{par * 2 + 1}"
            for h in range(2):
                k.op("g", lambda g, ig=ig, h=h: g.dma_start(
                        out=wgb[h][:], in_=wg_d[ig, :, h * 16:(h + 1) * 16, :]),
                     writes=(f"wgb{h}",), dma=True)
                k.op("g", lambda g, ig=ig, h=h: g.dma_start(
                        out=wub[h][:], in_=wu_d[ig, :, h * 16:(h + 1) * 16, :]),
                     writes=(f"wub{h}",), dma=True)
            for h in range(2):
                def gu(t, h=h, pg=pg, pu=pu):
                    last = None
                    for hh in range(16):
                        ht = h * 16 + hh
                        st_ = ht == 0
                        sp = ht == HT - 1
                        for rc in range(NRC):
                            w = min(512, T - rc * 512)
                            rhs = atfb[:, ht, rc * 512:rc * 512 + w]
                            t.matmul(pg[:, rc * 512:rc * 512 + w], wgb[h][:, hh, :],
                                     rhs, start=st_, stop=sp)
                            last = t.matmul(pu[:, rc * 512:rc * 512 + w], wub[h][:, hh, :],
                                            rhs, start=st_, stop=sp)
                    return last
                k.op("t", gu, reads=(f"wgb{h}", f"wub{h}", "atfb"), writes=(pgk, puk))
            sp_ = ig % 2
            k.op("s", lambda s, ig=ig, pg=pg, sp_=sp_: s.activation(
                    out=silu_s[sp_][:], in_=pg[:, 0:T], func=AF.Silu,
                    scale=sg_sb[:, ig:ig + 1]),
                 reads=(pgk, "sg", "cst"), writes=(f"silu{sp_}",))
            k.op("v", lambda v, ig=ig, pu=pu, sp_=sp_: v.scalar_tensor_tensor(
                    out=hid[:, ig, :], in0=pu[:, 0:HIDT], scalar=su_sb[:, ig:ig + 1],
                    in1=silu_s[sp_][:, 0:HIDT], op0=ALU.mult, op1=ALU.mult),
                 reads=(puk, "su", f"silu{sp_}"), writes=(f"hid{ig % 4}",))
            if NHB > 1:
                k.op("v", lambda v, ig=ig, pu=pu, sp_=sp_: v.scalar_tensor_tensor(
                        out=hstg[sp_][:], in0=pu[:, HIDT:T], scalar=su_sb[:, ig:ig + 1],
                        in1=silu_s[sp_][:, HIDT:T], op0=ALU.mult, op1=ALU.mult),
                     reads=(puk, "su", f"silu{sp_}"), writes=(f"hstg{sp_}",))
                k.op("y", lambda y, ig=ig, sp_=sp_: y.dma_start(
                        out=hb_d[ig], in_=hstg[sp_][:]),
                     reads=(f"hstg{sp_}",), writes=("hbw",), dma=True)
        k.barrier()
        s_gu.close()

        s_dn = ExitStack()
        sdb = s_dn.enter_context(nc.sbuf_tensor("sdb_sb", [128, L], F32))
        wdb = [s_dn.enter_context(nc.sbuf_tensor(f"wdb{p}", [128, 8, 512], BF16)) for p in range(3)]
        dstage = [s_dn.enter_context(nc.sbuf_tensor(f"dst{p}", [128, 512], F32)) for p in range(2)]
        sqd = s_dn.enter_context(nc.sbuf_tensor("sqd", [128, 512], F32))
        dnc = [s_dn.enter_context(nc.sbuf_tensor(f"dnc{p}", [128, 512], F32)) for p in range(2)]
        rc_ = [s_dn.enter_context(nc.sbuf_tensor(f"rc{p}", [128, 512], F32)) for p in range(2)]
        dps = [s_dn.enter_context(nc.psum_tensor(f"dps{q}", [128, 512], F32)) for q in range(8)]
        k.op("y", lambda y: y.dma_start(out=sdb[:], in_=sdb_d[:]), writes=("sdb",), dma=True)

        def emit_ln4(tta):
            rows = slice(tta * 128, (tta + 1) * 128)
            m, rstd = emit_rstd_from(k, stats, lnacc[:, tta:tta + 1],
                                     lnacc[:, NT + tta:NT + tta + 1], L)
            for hc in range(8):
                cp = hc % 2
                cols = slice(hc * 512, (hc + 1) * 512)
                k.op("s", lambda s, rows=rows, cols=cols, cp=cp: s.dma_start(
                        out=dnc[cp][:], in_=dn_d[rows, cols]),
                     reads=("dnw",), writes=(f"dnc{cp}",), dma=True)
                k.op("s", lambda s, rows=rows, cols=cols, cp=cp: s.dma_start(
                        out=rc_[cp][:], in_=r_d[rows, cols]),
                     writes=(f"rc{cp}",), dma=True)
                k.op("v", lambda v, cp=cp, m=m, rstd=rstd: v.tensor_scalar(
                        out=dnc[cp][:], in0=dnc[cp][:], scalar1=m, scalar2=rstd,
                        op0=ALU.subtract, op1=ALU.mult),
                     reads=(f"dnc{cp}", "stm", "strstd"), writes=(f"dnc{cp}",))
                k.op("v", lambda v, cp=cp: v.tensor_add(dnc[cp][:], dnc[cp][:], rc_[cp][:]),
                     reads=(f"dnc{cp}", f"rc{cp}"), writes=(f"dnc{cp}",))
                k.op("y", lambda y, rows=rows, cols=cols, cp=cp: y.dma_start(
                        out=y_d[rows, cols], in_=dnc[cp][:]),
                     reads=(f"dnc{cp}",), writes=("yw",), dma=True)

        for half in range(NHB):
            for hc in range(8):
                par = hc % 2
                pd = [dps[par * 4 + q] for q in range(NTT)]
                pdkeys = tuple(f"dps{par * 4 + q}" for q in range(NTT))
                for io in range(11):
                    cnt = min(8, ITI - io * 8)
                    wp = (hc * 11 + io) % 3
                    k.op("g", lambda g, hc=hc, io=io, cnt=cnt, wp=wp: g.dma_start(
                            out=wdb[wp][:, 0:cnt, :],
                            in_=wd_d[io * 8:io * 8 + cnt, :, hc * 512:(hc + 1) * 512]
                                .rearrange("a p c -> p a c")),
                         writes=(f"wdb{wp}",), dma=True)

                    def dn(t, io=io, cnt=cnt, wp=wp, pd=pd):
                        last = None
                        for ii in range(cnt):
                            i = io * 8 + ii
                            for tt in range(NTT):
                                last = t.matmul(pd[tt][:, 0:512],
                                                hid[:, i, tt * 128:(tt + 1) * 128],
                                                wdb[wp][:, ii, :],
                                                start=(i == 0), stop=(i == ITI - 1))
                        return last
                    k.op("t", dn, reads=(f"wdb{wp}",) + tuple(f"hid{q}" for q in range(4)),
                         writes=pdkeys)
                for tt in range(NTT):
                    tta = half * NTT + tt
                    dp = tt % 2
                    k.op("v", lambda v, hc=hc, tt=tt, dp=dp, pd=pd: v.tensor_tensor(
                            out=dstage[dp][:], in0=pd[tt][:, 0:512],
                            in1=sdb[:, hc * 512:(hc + 1) * 512], op=ALU.mult),
                         reads=(pdkeys[tt], "sdb"), writes=(f"dst{dp}",))
                    k.op("y", lambda y, tta=tta, hc=hc, dp=dp: y.dma_start(
                            out=dn_d[tta * 128:(tta + 1) * 128, hc * 512:(hc + 1) * 512],
                            in_=dstage[dp][:]),
                         reads=(f"dst{dp}",), writes=("dnw",), dma=True)
                    k.op("v", lambda v, tta=tta, dp=dp: v.reduce_sum(
                            lnacc[:, 2 * NT + tta:2 * NT + tta + 1], dstage[dp][:], axis=AX),
                         reads=(f"dst{dp}",), writes=("lntmp",))
                    if hc == 0:
                        k.op("v", lambda v, tta=tta: v.tensor_copy(
                                out=lnacc[:, tta:tta + 1],
                                in_=lnacc[:, 2 * NT + tta:2 * NT + tta + 1]),
                             reads=("lntmp",), writes=("lnacc",))
                    else:
                        k.op("v", lambda v, tta=tta: v.tensor_add(
                                lnacc[:, tta:tta + 1], lnacc[:, tta:tta + 1],
                                lnacc[:, 2 * NT + tta:2 * NT + tta + 1]),
                             reads=("lntmp", "lnacc"), writes=("lnacc",))
                    k.op("s", lambda s, dp=dp: s.activation(
                            out=sqd[:], in_=dstage[dp][:], func=AF.Square,
                            accum_out=stats[:, 6:7]),
                         reads=(f"dst{dp}", "cst"), writes=("lntmp2",))
                    if hc == 0:
                        k.op("v", lambda v, tta=tta: v.tensor_copy(
                                out=lnacc[:, NT + tta:NT + tta + 1], in_=stats[:, 6:7]),
                             reads=("lntmp2",), writes=("lnacc",))
                    else:
                        k.op("v", lambda v, tta=tta: v.tensor_add(
                                lnacc[:, NT + tta:NT + tta + 1],
                                lnacc[:, NT + tta:NT + tta + 1], stats[:, 6:7]),
                             reads=("lntmp2", "lnacc"), writes=("lnacc",))
            if half + 1 < NHB:
                for ch in range(8):
                    i0 = (ITI * ch) // 8
                    i1 = (ITI * (ch + 1)) // 8
                    k.op("y", lambda y, i0=i0, i1=i1: y.dma_start(
                            out=hid[:, i0:i1, :],
                            in_=hb_d[i0:i1, :, :].rearrange("a p c -> p a c")),
                         reads=("hbw",) + tuple(f"hid{q}" for q in range(4)),
                         writes=tuple(f"hid{q}" for q in range(4)), dma=True)
            for tt in range(NTT):
                emit_ln4(half * NTT + tt)

        k.barrier()
        s_dn.close()
        ffn.close()
        with nc.Block() as block:
            k.replay(block)

    nc.finalize()
    return nc


_NC_CACHE = {}


def _get_nc(NT=8):
    if NT not in _NC_CACHE:
        _NC_CACHE[NT] = build_nc(NT)
    return _NC_CACHE[NT]


def prep_weights(w_gate, w_up, w_down, s_gate, s_up, s_down):
    wg = np.ascontiguousarray(np.asarray(w_gate, dtype=np.int8)
                              .reshape(ITI, 128, HT, 128).transpose(0, 3, 2, 1))
    wu = np.ascontiguousarray(np.asarray(w_up, dtype=np.int8)
                              .reshape(ITI, 128, HT, 128).transpose(0, 3, 2, 1))
    wd = np.ascontiguousarray(np.asarray(w_down, dtype=np.int8).T.reshape(ITI, 128, L))
    sg = np.ascontiguousarray(np.asarray(s_gate, dtype=np.float32).reshape(ITI, 128).T)
    su = np.ascontiguousarray(np.asarray(s_up, dtype=np.float32).reshape(ITI, 128).T)
    sdb = np.ascontiguousarray(
        np.broadcast_to(np.asarray(s_down, dtype=np.float32)[None, :], (128, L)))
    return wg, wu, wd, sg, su, sdb


def kernel(L0_attn_x, L0_ffn_w_gate, L0_ffn_s_gate, L0_ffn_w_up, L0_ffn_s_up,
           L0_ffn_w_down, L0_ffn_s_down):
    x = np.ascontiguousarray(np.asarray(L0_attn_x, dtype=np.float32)).reshape(2 * L, L)
    wg, wu, wd, sg, su, sdb = prep_weights(L0_ffn_w_gate, L0_ffn_w_up, L0_ffn_w_down,
                                           L0_ffn_s_gate, L0_ffn_s_up, L0_ffn_s_down)
    cos_t, sin_t, icos_t, ident = host_tables()
    nc = _get_nc(8)
    T = 1024
    in_maps = []
    for c in range(NCORES):
        in_maps.append(dict(
            x=np.ascontiguousarray(x[c * T:(c + 1) * T]),
            wg=wg, wu=wu, wd=wd, sg=sg, su=su, sdb=sdb,
            cos_t=cos_t, sin_t=sin_t, icos_t=icos_t, ident=ident,
        ))
    res = run_bass_kernel_spmd(nc, in_maps, core_ids=list(range(NCORES)))
    out = np.concatenate([res.results[c]["y"] for c in range(NCORES)], axis=0)
    return out.reshape(2, L, L)


# version 10
# speedup vs baseline: 1.0411x; 1.0411x over previous
import functools
import numpy as np
import ml_dtypes
from contextlib import ExitStack

import concourse.bass as bass
import concourse.mybir as mybir
from concourse.bass_utils import run_bass_kernel_spmd

F32 = mybir.dt.float32
BF16 = mybir.dt.bfloat16
I8 = mybir.dt.int8
U32 = mybir.dt.uint32
AF = mybir.ActivationFunctionType
ALU = mybir.AluOpType
AX = mybir.AxisListType.X

L = 4096
KF = 2049
KC = 17
HT = 32
ITI = 86
INTER = 11008
EPS = 1e-5
TOPK = 8
NROLL = 1
NCORES = 8


@functools.lru_cache(maxsize=1)
def host_tables():
    bf = ml_dtypes.bfloat16
    t = np.arange(L, dtype=np.float64)
    k = np.arange(KC * 128, dtype=np.float64)
    ang = 2.0 * np.pi / L
    KT = np.outer(t, k)
    cosf = np.cos(ang * KT)
    sinf = np.sin(ang * KT)
    cosf[:, KF:] = 0.0
    sinf[:, KF:] = 0.0
    cos_t = np.ascontiguousarray(cosf.reshape(HT, 128, KC, 128).transpose(2, 0, 1, 3).astype(bf))
    sin_t = np.ascontiguousarray(sinf.reshape(HT, 128, KC, 128).transpose(2, 0, 1, 3).astype(bf))
    ck = np.full(KC * 128, 2.0); ck[0] = 1.0; ck[2048] = 1.0; ck[KF:] = 0.0
    icos = (ck[:, None] * np.cos(ang * np.outer(k, t))) / L
    icos[KF:, :] = 0.0
    icos_t = np.ascontiguousarray(icos.reshape(KC, 128, L).astype(bf))
    ident = np.eye(128, dtype=bf)
    return cos_t, sin_t, icos_t, ident


class BK:

    ENGS = ("v", "s", "t", "g", "y")

    def __init__(self, nc, st):
        self.nc = nc
        self.streams = {e: [] for e in self.ENGS}
        self.sems = {}
        self.lastw = {}
        self.buf = {}
        self.st = st
        for e in self.ENGS:
            self._mksem("S" + e)
        self.dma_pool = {}
        self.dma_rr = {}
        for e in ("y", "g", "s"):
            names = []
            for i in range(8):
                n = f"D{e}{i}"
                self._mksem(n)
                names.append(n)
            self.dma_pool[e] = names
            self.dma_rr[e] = 0

    def _mksem(self, name):
        h = self.st.enter_context(self.nc.semaphore(name))
        self.sems[name] = [h, 0]

    def _wait(self, e, tok):
        if tok is None:
            return
        name, cnt = tok
        if self.lastw.get((e, name), 0) >= cnt:
            return
        self.lastw[(e, name)] = cnt
        h = self.sems[name][0]
        self.streams[e].append(lambda eng, h=h, cnt=cnt: eng.wait_ge(h, cnt))

    def op(self, e, build, reads=(), writes=(), deps=(), dma=False):
        toks = list(deps)
        for b in reads:
            s = self.buf.get(b)
            if s and s["w"]:
                toks.append(s["w"])
        for b in writes:
            s = self.buf.get(b)
            if s:
                if s["w"]:
                    toks.append(s["w"])
                toks.extend(s["r"])
        for tok in toks:
            self._wait(e, tok)
        if dma:
            pool = self.dma_pool[e]
            name = pool[self.dma_rr[e] % len(pool)]
            self.dma_rr[e] += 1
            srec = self.sems[name]
            self._wait(e, (name, srec[1]))
            srec[1] += 16
        else:
            name = "S" + e
            srec = self.sems[name]
            srec[1] += 1
        cnt = srec[1]
        h = srec[0]
        inc = 16 if dma else 1
        self.streams[e].append(
            lambda eng, build=build, h=h, inc=inc: build(eng).then_inc(h, inc))
        tok = (name, cnt)
        for b in reads:
            self.buf.setdefault(b, {"w": None, "r": []})["r"].append(tok)
        for b in writes:
            self.buf[b] = {"w": tok, "r": []}
        return tok

    def barrier(self):
        toks = [(n, s[1]) for n, s in self.sems.items() if s[1] > 0]
        for e in self.ENGS:
            for tok in toks:
                self._wait(e, tok)
        for b in self.buf.values():
            b["r"] = []

    def replay(self, block):
        for e, meth in (("v", block.vector), ("s", block.scalar), ("t", block.tensor),
                        ("g", block.gpsimd), ("y", block.sync)):
            stream = self.streams[e]
            if stream:
                meth(lambda eng, stream=stream: [f(eng) for f in stream])


def emit_ln_bn(k, stats, bnb, src_key, src_ap, sx=""):
    m = stats[:, 0:1]; rstd = stats[:, 1:2]; tmp = stats[:, 5:6]
    for c in range(8):
        k.op("v", lambda v, c=c: v.bn_stats(bnb[:, c * 6:(c + 1) * 6],
                                            src_ap[:, c * 512:(c + 1) * 512]),
             reads=(src_key,), writes=("bnb" + sx,))
    k.op("v", lambda v: v.bn_aggr(stats[:, 0:2], bnb[:]), reads=("bnb" + sx,),
         writes=("stm" + sx, "stvar" + sx))
    k.op("s", lambda s: s.activation(out=tmp, in_=rstd, func=AF.Ln, bias=EPS),
         reads=("stvar" + sx, "cst"), writes=("sttmp" + sx,))
    k.op("s", lambda s: s.activation(out=rstd, in_=tmp, func=AF.Exp, scale=-0.5),
         reads=("sttmp" + sx, "cst"), writes=("strstd" + sx, "stvar" + sx))
    return m, rstd


def emit_rstd_from(k, stats, s1ap, s2ap, n):
    m = stats[:, 0:1]; rstd = stats[:, 1:2]
    msq = stats[:, 4:5]; var = stats[:, 5:6]
    k.op("v", lambda v: v.tensor_scalar_mul(m, s1ap, 1.0 / n), reads=("lnacc",), writes=("stm",))
    k.op("v", lambda v: v.tensor_mul(msq, m, m), reads=("stm",), writes=("stmsq",))
    k.op("v", lambda v: v.scalar_tensor_tensor(out=var, in0=s2ap, scalar=1.0 / n, in1=msq,
                                               op0=ALU.mult, op1=ALU.subtract),
         reads=("lnacc", "stmsq"), writes=("stvar",))
    k.op("s", lambda s: s.activation(out=var, in_=var, func=AF.Ln, bias=EPS),
         reads=("stvar", "cst"), writes=("stvar",))
    k.op("s", lambda s: s.activation(out=rstd, in_=var, func=AF.Exp, scale=-0.5),
         reads=("stvar", "cst"), writes=("strstd",))
    return m, rstd


def build_nc(NT=8):
    T = NT * 128
    RNC = 2 if NT >= 2 else 1
    NPASS = NT // RNC
    NRC = (T + 511) // 512
    HIDT = min(512, T)
    NHB = T // HIDT
    NTT = HIDT // 128
    PW = max(512, 512 * NRC)

    nc = bass.Bass("TRN2", target_bir_lowering=False)
    x_d = nc.dram_tensor("x", [T, L], F32, kind="ExternalInput")
    wg_d = nc.dram_tensor("wg", [ITI, 128, HT, 128], I8, kind="ExternalInput")
    wu_d = nc.dram_tensor("wu", [ITI, 128, HT, 128], I8, kind="ExternalInput")
    wd_d = nc.dram_tensor("wd", [ITI, 128, L], I8, kind="ExternalInput")
    sg_d = nc.dram_tensor("sg", [128, ITI], F32, kind="ExternalInput")
    su_d = nc.dram_tensor("su", [128, ITI], F32, kind="ExternalInput")
    sdb_d = nc.dram_tensor("sdb", [128, L], F32, kind="ExternalInput")
    cos_d = nc.dram_tensor("cos_t", [KC, HT, 128, 128], BF16, kind="ExternalInput")
    sin_d = nc.dram_tensor("sin_t", [KC, HT, 128, 128], BF16, kind="ExternalInput")
    icos_d = nc.dram_tensor("icos_t", [KC, 128, L], BF16, kind="ExternalInput")
    id_d = nc.dram_tensor("ident", [128, 128], BF16, kind="ExternalInput")
    y_d = nc.dram_tensor("y", [T, L], F32, kind="ExternalOutput")

    a2_d = nc.dram_tensor("a2_scr", [T, 2 * L], F32, kind="Internal")
    a2b_d = nc.dram_tensor("a2b_scr", [T, 2 * L], BF16, kind="Internal")
    r_d = nc.dram_tensor("r_scr", [T, L], F32, kind="Internal")
    dn_d = nc.dram_tensor("dn_scr", [T, L], F32, kind="Internal")
    atf_d = nc.dram_tensor("atf_scr", [HT, 128, T], BF16, kind="Internal")
    hb_d = nc.dram_tensor("hb_scr", [ITI, 128, HIDT], BF16, kind="Internal") if NHB > 1 else None

    with ExitStack() as st:
        k = BK(nc, st)
        ident = st.enter_context(nc.sbuf_tensor("identb", [128, 128], BF16))
        stats2 = [st.enter_context(nc.sbuf_tensor(f"stats{q}", [128, 8], F32)) for q in range(2)]
        stats = stats2[0]
        bnb2 = [st.enter_context(nc.sbuf_tensor(f"bnb{q}", [128, 6 * 8], F32)) for q in range(2)]
        sg_sb = st.enter_context(nc.sbuf_tensor("sg_sb", [128, ITI], F32))
        su_sb = st.enter_context(nc.sbuf_tensor("su_sb", [128, ITI], F32))
        top8 = [st.enter_context(nc.sbuf_tensor(f"top8{q}", [128, 8], BF16)) for q in range(2)]
        t8f = [st.enter_context(nc.sbuf_tensor(f"t8f{q}", [128, 8], F32)) for q in range(2)]
        idx8 = [st.enter_context(nc.sbuf_tensor(f"idx8{q}", [128, 8], U32)) for q in range(2)]
        smb = [st.enter_context(nc.sbuf_tensor(f"smb{q}", [128, 8], F32)) for q in range(2)]
        w8 = [st.enter_context(nc.sbuf_tensor(f"w8b{q}", [128, 8], F32)) for q in range(2)]
        ssum = [st.enter_context(nc.sbuf_tensor(f"ssumb{q}", [128, 2], F32)) for q in range(2)]
        iot = [st.enter_context(nc.sbuf_tensor(f"iotb{q}", [128, 8], U32)) for q in range(2)]
        gidx = [st.enter_context(nc.sbuf_tensor(f"gidxb{q}", [128, 8], U32)) for q in range(2)]
        lnacc = st.enter_context(nc.sbuf_tensor("lnacc", [128, 3 * NT], F32))
        cst = st.enter_context(nc.sbuf_tensor("cstb", [128, 4], F32))
        for ci, cv in enumerate((EPS, -0.5, 1.0, 0.0)):
            nc.const_aps.register(F32, cv, cst[:, ci:ci + 1])
        k.op("v", lambda v: v.memset(cst[:, 0:1], EPS), writes=("cst",))
        k.op("v", lambda v: v.memset(cst[:, 1:2], -0.5), reads=("cst",), writes=("cst",))
        k.op("v", lambda v: v.memset(cst[:, 2:3], 1.0), reads=("cst",), writes=("cst",))
        k.op("v", lambda v: v.memset(cst[:, 3:4], 0.0), reads=("cst",), writes=("cst",))
        k.barrier()

        k.op("y", lambda y: y.dma_start(out=ident[:], in_=id_d[:]), writes=("ident",), dma=True)
        k.op("y", lambda y: y.dma_start(out=sg_sb[:], in_=sg_d[:]), writes=("sg",), dma=True)
        k.op("y", lambda y: y.dma_start(out=su_sb[:], in_=su_d[:]), writes=("su",), dma=True)

        s_aT = ExitStack()
        aT = s_aT.enter_context(nc.sbuf_tensor("aT", [128, HT, T], BF16))
        s_pT = ExitStack()
        pT = s_pT.enter_context(nc.sbuf_tensor("pT", [128, KC, T], BF16, side="right"))
        s_a1 = ExitStack()
        xs = [s_a1.enter_context(nc.sbuf_tensor(f"xs{p}", [128, L], F32)) for p in range(2)]
        abf = [s_a1.enter_context(nc.sbuf_tensor(f"abf{p}", [128, L], BF16)) for p in range(2)]
        sq = s_a1.enter_context(nc.sbuf_tensor("sq", [128, 2048], F32))
        cosb = [s_a1.enter_context(nc.sbuf_tensor(f"cosb{p}", [128, HT, 128], BF16)) for p in range(2)]
        sinb = [s_a1.enter_context(nc.sbuf_tensor(f"sinb{p}", [128, HT, 128], BF16)) for p in range(2)]
        tp_ps = [s_a1.enter_context(nc.psum_tensor(f"tp{p}", [128, 1024], BF16)) for p in range(2)]
        re_ps = [s_a1.enter_context(nc.psum_tensor(f"re{p}", [128, 512], F32)) for p in range(2)]
        im_ps = [s_a1.enter_context(nc.psum_tensor(f"im{p}", [128, 512], F32)) for p in range(2)]
        HALF = (NT + 1) // 2

        def emit_a1(i):
            p = i % 2
            rows = slice(i * 128, (i + 1) * 128)
            k.op("y", lambda y, rows=rows, p=p: y.dma_start(out=xs[p][:], in_=x_d[rows, :]),
                 writes=(f"xs{p}",), dma=True)
            m, rstd = emit_ln_bn(k, stats2[p], bnb2[p], f"xs{p}", xs[p][:], sx=str(p))
            k.op("v", lambda v, p=p, m=m, rstd=rstd: v.tensor_scalar(
                    out=xs[p][:], in0=xs[p][:], scalar1=m, scalar2=rstd,
                    op0=ALU.subtract, op1=ALU.mult),
                 reads=(f"xs{p}", f"stm{p}", f"strstd{p}"), writes=(f"xs{p}",))
            k.op("y", lambda y, rows=rows, p=p: y.dma_start(out=a2_d[rows, 0:L], in_=xs[p][:]),
                 reads=(f"xs{p}",), writes=("a2w",), dma=True)
            k.op("y", lambda y, rows=rows, p=p: y.dma_start(out=a2_d[rows, L:2 * L], in_=xs[p][:]),
                 reads=(f"xs{p}",), writes=("a2w",), dma=True)
            k.op("v", lambda v, p=p: v.tensor_copy(out=abf[p][:], in_=xs[p][:]),
                 reads=(f"xs{p}",), writes=(f"abf{p}",))
            k.op("g", lambda g, rows=rows, p=p: g.dma_start(out=a2b_d[rows, 0:L], in_=abf[p][:]),
                 reads=(f"abf{p}",), writes=("a2bw",), dma=True)
            k.op("g", lambda g, rows=rows, p=p: g.dma_start(out=a2b_d[rows, L:2 * L], in_=abf[p][:]),
                 reads=(f"abf{p}",), writes=("a2bw",), dma=True)

        def emit_tp(i, akey):
            p = i % 2
            for tc in range(HT):
                pp = tc % 2
                k.op("t", lambda t, tc=tc, pp=pp, p=p: t.transpose(
                        tp_ps[pp][:, 0:128], abf[p][:, tc * 128:(tc + 1) * 128], ident[:]),
                     reads=(f"abf{p}", "ident"), writes=(f"tp{pp}",))
                k.op("s", lambda sc, tc=tc, pp=pp, i=i: sc.copy(
                        out=aT[:, tc, i * 128:(i + 1) * 128], in_=tp_ps[pp][:, 0:128]),
                     reads=(f"tp{pp}",), writes=(akey,))

        def load_tab(kc):
            p = kc % 2
            k.op("s", lambda sc, kc=kc, p=p: sc.dma_start(
                    out=cosb[p][:], in_=cos_d[kc].rearrange("a p c -> p a c")),
                 writes=(f"cosb{p}",), dma=True)
            k.op("s", lambda sc, kc=kc, p=p: sc.dma_start(
                    out=sinb[p][:], in_=sin_d[kc].rearrange("a p c -> p a c")),
                 writes=(f"sinb{p}",), dma=True)

        def emit_fwd(rc, akey):
            w = min(512, T - rc * 512)
            if rc > 0:
                load_tab(0)
            for kc in range(KC):
                p = kc % 2
                if kc + 1 < KC:
                    load_tab(kc + 1)

                def fwd(t, p=p, rc=rc, w=w):
                    last = None
                    for tc in range(HT):
                        st_ = (tc == 0)
                        sp = (tc == HT - 1)
                        t.matmul(re_ps[p][:, 0:w], cosb[p][:, tc, :],
                                 aT[:, tc, rc * 512:rc * 512 + w], start=st_, stop=sp)
                        last = t.matmul(im_ps[p][:, 0:w], sinb[p][:, tc, :],
                                        aT[:, tc, rc * 512:rc * 512 + w], start=st_, stop=sp)
                    return last
                k.op("t", fwd, reads=(f"cosb{p}", f"sinb{p}", akey), writes=(f"re{p}", f"im{p}"))
                k.op("s", lambda sc, p=p, w=w: sc.activation(
                        out=sq[:, 0:w], in_=re_ps[p][:, 0:w], func=AF.Square),
                     reads=(f"re{p}", "cst"), writes=("sqa",))
                k.op("s", lambda sc, p=p, w=w: sc.activation(
                        out=sq[:, 1024:1024 + w], in_=im_ps[p][:, 0:w], func=AF.Square),
                     reads=(f"im{p}", "cst"), writes=("sqb",))
                k.op("v", lambda v, kc=kc, rc=rc, w=w: v.tensor_add(
                        pT[:, kc, rc * 512:rc * 512 + w], sq[:, 0:w], sq[:, 1024:1024 + w]),
                     reads=("sqa", "sqb"), writes=("pT",))

        load_tab(0)
        for i in range(NT):
            emit_a1(i)
            emit_tp(i, "aTa" if i < HALF else "aTb")
        emit_fwd(0, "aTa")
        if NT > 4:
            emit_fwd(1, "aTb")
        k.barrier()
        s_a1.close()
        s_aT.close()

        s_corr = ExitStack()
        corrF = s_corr.enter_context(nc.sbuf_tensor("corrF", [128, NT, L], BF16))
        s_att = ExitStack()
        HL = L // 4
        icosb = [s_att.enter_context(nc.sbuf_tensor(f"icosb{p}", [128, HL], BF16)) for p in range(3)]
        rollf = s_att.enter_context(nc.sbuf_tensor("rollf", [128, L], F32))
        rollb = ([s_att.enter_context(nc.sbuf_tensor(f"rollb{p}", [128, L], BF16))
                  for p in range(3)] if NROLL > 1 else [])
        da = [s_att.enter_context(nc.sbuf_tensor(f"da{q}", [128, L], F32)) for q in range(2)]
        stg = [s_att.enter_context(nc.sbuf_tensor(f"stg{p}", [128, HT, 128], BF16)) for p in range(2)]
        s_attp = ExitStack()
        cr_ps = [s_attp.enter_context(nc.psum_tensor(f"cr{r}", [128, HL], F32))
                 for r in range(2 * RNC)]

        def load_icos(n, seq):
            ps_, th, kc = seq[n]
            p = n % 3
            k.op("y", lambda y, kc=kc, th=th, p=p: y.dma_start(
                    out=icosb[p][:], in_=icos_d[kc, :, th * HL:(th + 1) * HL]),
                 writes=(f"icosb{p}",), dma=True)

        icos_seq = [(ps_, th, kc) for ps_ in range(NPASS) for th in range(4) for kc in range(KC)]
        load_icos(0, icos_seq)
        load_icos(1, icos_seq)
        nld = 2
        pend = []

        def emit_a456(i):
            q = i % 2
            sx = str(q)
            rows = slice(i * 128, (i + 1) * 128)
            k.op("v", lambda v, i=i, q=q: v.max(out=top8[q][:], in_=corrF[:, i, :]),
                 reads=(f"corr{i}",), writes=("top8" + sx,))
            k.op("v", lambda v, i=i, q=q: v.max_index(out=idx8[q][:], in_max=top8[q][:],
                                                 in_values=corrF[:, i, :]),
                 reads=("top8" + sx, f"corr{i}"), writes=("idx8" + sx,))
            k.op("v", lambda v, q=q: v.tensor_copy(out=t8f[q][:], in_=top8[q][:]),
                 reads=("top8" + sx,), writes=("t8f" + sx,))
            k.op("v", lambda v, q=q: v.tensor_scalar(out=smb[q][:], in0=t8f[q][:],
                                                scalar1=t8f[q][:, 0:1],
                                                scalar2=None, op0=ALU.subtract),
                 reads=("t8f" + sx,), writes=("smbb" + sx,))
            k.op("s", lambda s, q=q: s.activation(out=smb[q][:], in_=smb[q][:], func=AF.Exp),
                 reads=("smbb" + sx, "cst"), writes=("smbb" + sx,))
            k.op("v", lambda v, q=q: v.reduce_sum(ssum[q][:, 0:1], smb[q][:], axis=AX),
                 reads=("smbb" + sx,), writes=("ssum" + sx,))
            k.op("v", lambda v, q=q: v.reciprocal(ssum[q][:, 1:2], ssum[q][:, 0:1]),
                 reads=("ssum" + sx,), writes=("ssumr" + sx,))
            k.op("v", lambda v, q=q: v.tensor_scalar(out=w8[q][:], in0=smb[q][:],
                                                scalar1=ssum[q][:, 1:2],
                                                scalar2=None, op0=ALU.mult),
                 reads=("smbb" + sx, "ssumr" + sx), writes=("w8" + sx,))
            k.op("g", lambda g, i=i, q=q: g.iota(iot[q][:], pattern=[[0, 8]],
                                            base=i * 128 * 2 * L,
                                            channel_multiplier=2 * L),
                 writes=("iot" + sx,))
            k.op("v", lambda v, q=q: v.tensor_tensor(out=gidx[q][:], in0=idx8[q][:],
                                                in1=iot[q][:], op=ALU.add),
                 reads=("idx8" + sx, "iot" + sx), writes=("gidx" + sx,))
            k.op("g", lambda g, q=q: g.indirect_dma_start(
                    out=rollf[:], out_offset=None, in_=a2_d[:, :],
                    in_offset=bass.IndirectOffsetOnAxis(ap=gidx[q][:, 0:1], axis=1)),
                 reads=("gidx" + sx,), writes=("rollf",), dma=True)
            k.op("v", lambda v, q=q: v.tensor_scalar(
                    out=da[q][:], in0=rollf[:], scalar1=w8[q][:, 0:1], scalar2=None,
                    op0=ALU.mult),
                 reads=("rollf", "w8" + sx), writes=("da" + sx,))
            for j in range(1, NROLL):
                p = (i * (TOPK - 1) + j) % 3
                k.op("g", lambda g, j=j, p=p, q=q: g.indirect_dma_start(
                        out=rollb[p][:], out_offset=None, in_=a2b_d[:, :],
                        in_offset=bass.IndirectOffsetOnAxis(ap=gidx[q][:, j:j + 1], axis=1)),
                     reads=("gidx" + sx,), writes=(f"rollb{p}",), dma=True)
                k.op("v", lambda v, j=j, p=p, q=q: v.scalar_tensor_tensor(
                        out=da[q][:], in0=rollb[p][:], scalar=w8[q][:, j:j + 1], in1=da[q][:],
                        op0=ALU.mult, op1=ALU.add),
                     reads=(f"rollb{p}", "w8" + sx, "da" + sx), writes=("da" + sx,))
            m, rstd = emit_ln_bn(k, stats2[q], bnb2[q], "da" + sx, da[q][:], sx=sx)
            k.op("v", lambda v, q=q, m=m, rstd=rstd: v.tensor_scalar(
                    out=da[q][:], in0=da[q][:], scalar1=m, scalar2=rstd,
                    op0=ALU.subtract, op1=ALU.mult),
                 reads=("da" + sx, "stm" + sx, "strstd" + sx), writes=("da" + sx,))

        def emit_a456_B(i):
            q = i % 2
            sx = str(q)
            rows = slice(i * 128, (i + 1) * 128)
            k.op("g", lambda g, rows=rows, q=q: g.dma_start(
                    out=da[q][:], in_=x_d[rows, :], accum_op=ALU.add,
                    max_dma_last_dim=2048),
                 reads=("da" + sx,), writes=("da" + sx,), dma=True)
            k.op("g", lambda g, rows=rows, q=q: g.dma_start(out=r_d[rows, :], in_=da[q][:]),
                 reads=("da" + sx,), writes=("rw",), dma=True)
            m, rstd = emit_ln_bn(k, stats2[q], bnb2[q], "da" + sx, da[q][:], sx=sx)
            k.op("v", lambda v, i=i, q=q, m=m, rstd=rstd: v.tensor_scalar(
                    out=corrF[:, i, :], in0=da[q][:], scalar1=m, scalar2=rstd,
                    op0=ALU.subtract, op1=ALU.mult),
                 reads=("da" + sx, "stm" + sx, "strstd" + sx), writes=(f"corr{i}",))
        for ps_ in range(NPASS):
            for th in range(4):
                for kc in range(KC):
                    if nld < len(icos_seq):
                        load_icos(nld, icos_seq)
                        nld += 1
                    p = (ps_ * 4 * KC + th * KC + kc) % 3
                    cpar = ((ps_ * 4 + th) % 2) * RNC

                    def corr(t, ps_=ps_, kc=kc, p=p, cpar=cpar):
                        last = None
                        for rt in range(RNC):
                            row = (ps_ * RNC + rt) * 128
                            for tch in range(HL // 512):
                                last = t.matmul(cr_ps[cpar + rt][:, tch * 512:(tch + 1) * 512],
                                                pT[:, kc, row:row + 128],
                                                icosb[p][:, tch * 512:(tch + 1) * 512],
                                                start=(kc == 0), stop=(kc == KC - 1))
                        return last
                    k.op("t", corr, reads=(f"icosb{p}", "pT"),
                         writes=tuple(f"cr{cpar + rt}" for rt in range(RNC)))
                cpar = ((ps_ * 4 + th) % 2) * RNC
                for rt in range(RNC):
                    i = ps_ * RNC + rt
                    k.op("s", lambda sc, i=i, rt=rt, th=th, cpar=cpar: sc.copy(
                            out=corrF[:, i, th * HL:(th + 1) * HL], in_=cr_ps[cpar + rt][:]),
                         reads=(f"cr{cpar + rt}",), writes=(f"corr{i}",))
            for rt in range(RNC):
                i = ps_ * RNC + rt
                emit_a456(i)
                if pend:
                    emit_a456_B(pend.pop(0))
                pend.append(i)
        while pend:
            emit_a456_B(pend.pop(0))
        s_attp.close()
        s_tq = ExitStack()
        tq_ps = [s_tq.enter_context(nc.psum_tensor(f"tq{p}", [128, 1024], BF16)) for p in range(2)]
        for i in range(NT):
            sp2 = i % 2
            for tc in range(HT):
                pp = tc % 2
                k.op("t", lambda t, i=i, tc=tc, pp=pp: t.transpose(
                        tq_ps[pp][:, 0:128], corrF[:, i, tc * 128:(tc + 1) * 128], ident[:]),
                     reads=(f"corr{i}", "ident"), writes=(f"tq{pp}",))
                k.op("s", lambda s, tc=tc, pp=pp, sp2=sp2: s.copy(
                        out=stg[sp2][:, tc, :], in_=tq_ps[pp][:, 0:128]),
                     reads=(f"tq{pp}",), writes=(f"stg{sp2}",))
            for hf in range(2):
                k.op("y", lambda y, i=i, hf=hf, sp2=sp2: y.dma_start(
                        out=atf_d[hf * 16:(hf + 1) * 16, :, i * 128:(i + 1) * 128]
                            .rearrange("a p c -> p a c"),
                        in_=stg[sp2][:, hf * 16:(hf + 1) * 16, :]),
                     reads=(f"stg{sp2}",), writes=("atfw",), dma=True)
        k.barrier()
        s_tq.close()
        s_att.close()
        s_corr.close()
        s_pT.close()

        ffn = ExitStack()
        hid = ffn.enter_context(nc.sbuf_tensor("hid", [128, ITI, HIDT], BF16))
        s_gu = ExitStack()
        atfb = s_gu.enter_context(nc.sbuf_tensor("atfb", [128, HT, T], BF16))
        wgb = [s_gu.enter_context(nc.sbuf_tensor(f"wgb{h}", [128, 16, 128], BF16)) for h in range(2)]
        wub = [s_gu.enter_context(nc.sbuf_tensor(f"wub{h}", [128, 16, 128], BF16)) for h in range(2)]
        silu_s = [s_gu.enter_context(nc.sbuf_tensor(f"silu{p}", [128, T], F32)) for p in range(2)]
        hstg = [s_gu.enter_context(nc.sbuf_tensor(f"hstg{p}", [128, HIDT], BF16)) for p in range(2)]
        gps = [s_gu.enter_context(nc.psum_tensor(f"gps{q}", [128, PW], F32)) for q in range(4)]

        for hf in range(2):
            k.op("y", lambda y, hf=hf: y.dma_start(
                    out=atfb[:, hf * 16:(hf + 1) * 16, :],
                    in_=atf_d[hf * 16:(hf + 1) * 16, :, :].rearrange("a p c -> p a c")),
                 writes=("atfb",), dma=True)

        for ig in range(ITI):
            par = ig % 2
            pg = gps[par * 2 + 0]
            pu = gps[par * 2 + 1]
            pgk = f"gps{par * 2}"
            puk = f"gps{par * 2 + 1}"
            for h in range(2):
                k.op("g", lambda g, ig=ig, h=h: g.dma_start(
                        out=wgb[h][:], in_=wg_d[ig, :, h * 16:(h + 1) * 16, :]),
                     writes=(f"wgb{h}",), dma=True)
                k.op("g", lambda g, ig=ig, h=h: g.dma_start(
                        out=wub[h][:], in_=wu_d[ig, :, h * 16:(h + 1) * 16, :]),
                     writes=(f"wub{h}",), dma=True)
            for h in range(2):
                def gu(t, h=h, pg=pg, pu=pu):
                    last = None
                    for hh in range(16):
                        ht = h * 16 + hh
                        st_ = ht == 0
                        sp = ht == HT - 1
                        for rc in range(NRC):
                            w = min(512, T - rc * 512)
                            rhs = atfb[:, ht, rc * 512:rc * 512 + w]
                            t.matmul(pg[:, rc * 512:rc * 512 + w], wgb[h][:, hh, :],
                                     rhs, start=st_, stop=sp)
                            last = t.matmul(pu[:, rc * 512:rc * 512 + w], wub[h][:, hh, :],
                                            rhs, start=st_, stop=sp)
                    return last
                k.op("t", gu, reads=(f"wgb{h}", f"wub{h}", "atfb"), writes=(pgk, puk))
            sp_ = ig % 2
            k.op("s", lambda s, ig=ig, pg=pg, sp_=sp_: s.activation(
                    out=silu_s[sp_][:], in_=pg[:, 0:T], func=AF.Silu,
                    scale=sg_sb[:, ig:ig + 1]),
                 reads=(pgk, "sg", "cst"), writes=(f"silu{sp_}",))
            k.op("v", lambda v, ig=ig, pu=pu, sp_=sp_: v.scalar_tensor_tensor(
                    out=hid[:, ig, :], in0=pu[:, 0:HIDT], scalar=su_sb[:, ig:ig + 1],
                    in1=silu_s[sp_][:, 0:HIDT], op0=ALU.mult, op1=ALU.mult),
                 reads=(puk, "su", f"silu{sp_}"), writes=(f"hid{ig % 4}",))
            if NHB > 1:
                k.op("v", lambda v, ig=ig, pu=pu, sp_=sp_: v.scalar_tensor_tensor(
                        out=hstg[sp_][:], in0=pu[:, HIDT:T], scalar=su_sb[:, ig:ig + 1],
                        in1=silu_s[sp_][:, HIDT:T], op0=ALU.mult, op1=ALU.mult),
                     reads=(puk, "su", f"silu{sp_}"), writes=(f"hstg{sp_}",))
                k.op("y", lambda y, ig=ig, sp_=sp_: y.dma_start(
                        out=hb_d[ig], in_=hstg[sp_][:]),
                     reads=(f"hstg{sp_}",), writes=("hbw",), dma=True)
        k.barrier()
        s_gu.close()

        s_dn = ExitStack()
        sdb = s_dn.enter_context(nc.sbuf_tensor("sdb_sb", [128, L], F32))
        wdb = [s_dn.enter_context(nc.sbuf_tensor(f"wdb{p}", [128, 8, 512], BF16)) for p in range(3)]
        dstage = [s_dn.enter_context(nc.sbuf_tensor(f"dst{p}", [128, 512], F32)) for p in range(2)]
        sqd = s_dn.enter_context(nc.sbuf_tensor("sqd", [128, 512], F32))
        dnc = [s_dn.enter_context(nc.sbuf_tensor(f"dnc{p}", [128, 512], F32)) for p in range(2)]
        rc_ = [s_dn.enter_context(nc.sbuf_tensor(f"rc{p}", [128, 512], F32)) for p in range(2)]
        dps = [s_dn.enter_context(nc.psum_tensor(f"dps{q}", [128, 512], F32)) for q in range(8)]
        k.op("y", lambda y: y.dma_start(out=sdb[:], in_=sdb_d[:]), writes=("sdb",), dma=True)

        def emit_ln4(tta):
            rows = slice(tta * 128, (tta + 1) * 128)
            m, rstd = emit_rstd_from(k, stats, lnacc[:, tta:tta + 1],
                                     lnacc[:, NT + tta:NT + tta + 1], L)
            for hc in range(8):
                cp = hc % 2
                cols = slice(hc * 512, (hc + 1) * 512)
                k.op("s", lambda s, rows=rows, cols=cols, cp=cp: s.dma_start(
                        out=dnc[cp][:], in_=dn_d[rows, cols]),
                     reads=("dnw",), writes=(f"dnc{cp}",), dma=True)
                k.op("s", lambda s, rows=rows, cols=cols, cp=cp: s.dma_start(
                        out=rc_[cp][:], in_=r_d[rows, cols]),
                     writes=(f"rc{cp}",), dma=True)
                k.op("v", lambda v, cp=cp, m=m, rstd=rstd: v.tensor_scalar(
                        out=dnc[cp][:], in0=dnc[cp][:], scalar1=m, scalar2=rstd,
                        op0=ALU.subtract, op1=ALU.mult),
                     reads=(f"dnc{cp}", "stm", "strstd"), writes=(f"dnc{cp}",))
                k.op("v", lambda v, cp=cp: v.tensor_add(dnc[cp][:], dnc[cp][:], rc_[cp][:]),
                     reads=(f"dnc{cp}", f"rc{cp}"), writes=(f"dnc{cp}",))
                k.op("y", lambda y, rows=rows, cols=cols, cp=cp: y.dma_start(
                        out=y_d[rows, cols], in_=dnc[cp][:]),
                     reads=(f"dnc{cp}",), writes=("yw",), dma=True)

        for half in range(NHB):
            for hc in range(8):
                par = hc % 2
                pd = [dps[par * 4 + q] for q in range(NTT)]
                pdkeys = tuple(f"dps{par * 4 + q}" for q in range(NTT))
                for io in range(11):
                    cnt = min(8, ITI - io * 8)
                    wp = (hc * 11 + io) % 3
                    k.op("g", lambda g, hc=hc, io=io, cnt=cnt, wp=wp: g.dma_start(
                            out=wdb[wp][:, 0:cnt, :],
                            in_=wd_d[io * 8:io * 8 + cnt, :, hc * 512:(hc + 1) * 512]
                                .rearrange("a p c -> p a c")),
                         writes=(f"wdb{wp}",), dma=True)

                    def dn(t, io=io, cnt=cnt, wp=wp, pd=pd):
                        last = None
                        for ii in range(cnt):
                            i = io * 8 + ii
                            for tt in range(NTT):
                                last = t.matmul(pd[tt][:, 0:512],
                                                hid[:, i, tt * 128:(tt + 1) * 128],
                                                wdb[wp][:, ii, :],
                                                start=(i == 0), stop=(i == ITI - 1))
                        return last
                    k.op("t", dn, reads=(f"wdb{wp}",) + tuple(f"hid{q}" for q in range(4)),
                         writes=pdkeys)
                for tt in range(NTT):
                    tta = half * NTT + tt
                    dp = tt % 2
                    k.op("v", lambda v, hc=hc, tt=tt, dp=dp, pd=pd: v.tensor_tensor(
                            out=dstage[dp][:], in0=pd[tt][:, 0:512],
                            in1=sdb[:, hc * 512:(hc + 1) * 512], op=ALU.mult),
                         reads=(pdkeys[tt], "sdb"), writes=(f"dst{dp}",))
                    k.op("y", lambda y, tta=tta, hc=hc, dp=dp: y.dma_start(
                            out=dn_d[tta * 128:(tta + 1) * 128, hc * 512:(hc + 1) * 512],
                            in_=dstage[dp][:]),
                         reads=(f"dst{dp}",), writes=("dnw",), dma=True)
                    k.op("v", lambda v, tta=tta, dp=dp: v.reduce_sum(
                            lnacc[:, 2 * NT + tta:2 * NT + tta + 1], dstage[dp][:], axis=AX),
                         reads=(f"dst{dp}",), writes=("lntmp",))
                    if hc == 0:
                        k.op("v", lambda v, tta=tta: v.tensor_copy(
                                out=lnacc[:, tta:tta + 1],
                                in_=lnacc[:, 2 * NT + tta:2 * NT + tta + 1]),
                             reads=("lntmp",), writes=("lnacc",))
                    else:
                        k.op("v", lambda v, tta=tta: v.tensor_add(
                                lnacc[:, tta:tta + 1], lnacc[:, tta:tta + 1],
                                lnacc[:, 2 * NT + tta:2 * NT + tta + 1]),
                             reads=("lntmp", "lnacc"), writes=("lnacc",))
                    k.op("s", lambda s, dp=dp: s.activation(
                            out=sqd[:], in_=dstage[dp][:], func=AF.Square,
                            accum_out=stats[:, 6:7]),
                         reads=(f"dst{dp}", "cst"), writes=("lntmp2",))
                    if hc == 0:
                        k.op("v", lambda v, tta=tta: v.tensor_copy(
                                out=lnacc[:, NT + tta:NT + tta + 1], in_=stats[:, 6:7]),
                             reads=("lntmp2",), writes=("lnacc",))
                    else:
                        k.op("v", lambda v, tta=tta: v.tensor_add(
                                lnacc[:, NT + tta:NT + tta + 1],
                                lnacc[:, NT + tta:NT + tta + 1], stats[:, 6:7]),
                             reads=("lntmp2", "lnacc"), writes=("lnacc",))
            if half + 1 < NHB:
                for ch in range(8):
                    i0 = (ITI * ch) // 8
                    i1 = (ITI * (ch + 1)) // 8
                    k.op("y", lambda y, i0=i0, i1=i1: y.dma_start(
                            out=hid[:, i0:i1, :],
                            in_=hb_d[i0:i1, :, :].rearrange("a p c -> p a c")),
                         reads=("hbw",) + tuple(f"hid{q}" for q in range(4)),
                         writes=tuple(f"hid{q}" for q in range(4)), dma=True)
            for tt in range(NTT):
                emit_ln4(half * NTT + tt)

        k.barrier()
        s_dn.close()
        ffn.close()
        with nc.Block() as block:
            k.replay(block)

    nc.finalize()
    return nc


_NC_CACHE = {}


def _get_nc(NT=8):
    if NT not in _NC_CACHE:
        _NC_CACHE[NT] = build_nc(NT)
    return _NC_CACHE[NT]


def prep_weights(w_gate, w_up, w_down, s_gate, s_up, s_down):
    wg = np.ascontiguousarray(np.asarray(w_gate, dtype=np.int8)
                              .reshape(ITI, 128, HT, 128).transpose(0, 3, 2, 1))
    wu = np.ascontiguousarray(np.asarray(w_up, dtype=np.int8)
                              .reshape(ITI, 128, HT, 128).transpose(0, 3, 2, 1))
    wd = np.ascontiguousarray(np.asarray(w_down, dtype=np.int8).T.reshape(ITI, 128, L))
    sg = np.ascontiguousarray(np.asarray(s_gate, dtype=np.float32).reshape(ITI, 128).T)
    su = np.ascontiguousarray(np.asarray(s_up, dtype=np.float32).reshape(ITI, 128).T)
    sdb = np.ascontiguousarray(
        np.broadcast_to(np.asarray(s_down, dtype=np.float32)[None, :], (128, L)))
    return wg, wu, wd, sg, su, sdb


def kernel(L0_attn_x, L0_ffn_w_gate, L0_ffn_s_gate, L0_ffn_w_up, L0_ffn_s_up,
           L0_ffn_w_down, L0_ffn_s_down):
    x = np.ascontiguousarray(np.asarray(L0_attn_x, dtype=np.float32)).reshape(2 * L, L)
    wg, wu, wd, sg, su, sdb = prep_weights(L0_ffn_w_gate, L0_ffn_w_up, L0_ffn_w_down,
                                           L0_ffn_s_gate, L0_ffn_s_up, L0_ffn_s_down)
    cos_t, sin_t, icos_t, ident = host_tables()
    nc = _get_nc(8)
    T = 1024
    in_maps = []
    for c in range(NCORES):
        in_maps.append(dict(
            x=np.ascontiguousarray(x[c * T:(c + 1) * T]),
            wg=wg, wu=wu, wd=wd, sg=sg, su=su, sdb=sdb,
            cos_t=cos_t, sin_t=sin_t, icos_t=icos_t, ident=ident,
        ))
    res = run_bass_kernel_spmd(nc, in_maps, core_ids=list(range(NCORES)))
    out = np.concatenate([res.results[c]["y"] for c in range(NCORES)], axis=0)
    return out.reshape(2, L, L)


# version 11
# speedup vs baseline: 1.0675x; 1.0254x over previous
import functools
import numpy as np
import ml_dtypes
from contextlib import ExitStack

import concourse.bass as bass
import concourse.mybir as mybir
from concourse.bass_utils import run_bass_kernel_spmd

F32 = mybir.dt.float32
BF16 = mybir.dt.bfloat16
I8 = mybir.dt.int8
U32 = mybir.dt.uint32
AF = mybir.ActivationFunctionType
ALU = mybir.AluOpType
AX = mybir.AxisListType.X

L = 4096
KF = 2049
KC = 17
HT = 32
ITI = 86
INTER = 11008
EPS = 1e-5
TOPK = 8
NROLL = 1
NCORES = 8


@functools.lru_cache(maxsize=1)
def host_tables():
    bf = ml_dtypes.bfloat16
    t = np.arange(L, dtype=np.float64)
    k = np.arange(KC * 128, dtype=np.float64)
    ang = 2.0 * np.pi / L
    KT = np.outer(t, k)
    cosf = np.cos(ang * KT)
    sinf = np.sin(ang * KT)
    cosf[:, KF:] = 0.0
    sinf[:, KF:] = 0.0
    cos_t = np.ascontiguousarray(cosf.reshape(HT, 128, KC, 128).transpose(2, 0, 1, 3).astype(bf))
    sin_t = np.ascontiguousarray(sinf.reshape(HT, 128, KC, 128).transpose(2, 0, 1, 3).astype(bf))
    ck = np.full(KC * 128, 2.0); ck[0] = 1.0; ck[2048] = 1.0; ck[KF:] = 0.0
    icos = (ck[:, None] * np.cos(ang * np.outer(k, t))) / L
    icos[KF:, :] = 0.0
    icos_t = np.ascontiguousarray(icos.reshape(KC, 128, L).astype(bf))
    ident = np.eye(128, dtype=bf)
    return cos_t, sin_t, icos_t, ident


class BK:

    ENGS = ("v", "s", "t", "g", "y")

    def __init__(self, nc, st):
        self.nc = nc
        self.streams = {e: [] for e in self.ENGS}
        self.sems = {}
        self.lastw = {}
        self.buf = {}
        self.st = st
        for e in self.ENGS:
            self._mksem("S" + e)
        self.dma_pool = {}
        self.dma_rr = {}
        for e in ("y", "g", "s"):
            names = []
            for i in range(8):
                n = f"D{e}{i}"
                self._mksem(n)
                names.append(n)
            self.dma_pool[e] = names
            self.dma_rr[e] = 0

    def _mksem(self, name):
        h = self.st.enter_context(self.nc.semaphore(name))
        self.sems[name] = [h, 0]

    def _wait(self, e, tok):
        if tok is None:
            return
        name, cnt = tok
        if self.lastw.get((e, name), 0) >= cnt:
            return
        self.lastw[(e, name)] = cnt
        h = self.sems[name][0]
        self.streams[e].append(lambda eng, h=h, cnt=cnt: eng.wait_ge(h, cnt))

    def op(self, e, build, reads=(), writes=(), deps=(), dma=False):
        toks = list(deps)
        for b in reads:
            s = self.buf.get(b)
            if s and s["w"]:
                toks.append(s["w"])
        for b in writes:
            s = self.buf.get(b)
            if s:
                if s["w"]:
                    toks.append(s["w"])
                toks.extend(s["r"])
        for tok in toks:
            self._wait(e, tok)
        if dma:
            pool = self.dma_pool[e]
            name = pool[self.dma_rr[e] % len(pool)]
            self.dma_rr[e] += 1
            srec = self.sems[name]
            self._wait(e, (name, srec[1]))
            srec[1] += 16
        else:
            name = "S" + e
            srec = self.sems[name]
            srec[1] += 1
        cnt = srec[1]
        h = srec[0]
        inc = 16 if dma else 1
        self.streams[e].append(
            lambda eng, build=build, h=h, inc=inc: build(eng).then_inc(h, inc))
        tok = (name, cnt)
        for b in reads:
            self.buf.setdefault(b, {"w": None, "r": []})["r"].append(tok)
        for b in writes:
            self.buf[b] = {"w": tok, "r": []}
        return tok

    def barrier(self):
        toks = [(n, s[1]) for n, s in self.sems.items() if s[1] > 0]
        for e in self.ENGS:
            for tok in toks:
                self._wait(e, tok)
        for b in self.buf.values():
            b["r"] = []

    def replay(self, block):
        for e, meth in (("v", block.vector), ("s", block.scalar), ("t", block.tensor),
                        ("g", block.gpsimd), ("y", block.sync)):
            stream = self.streams[e]
            if stream:
                meth(lambda eng, stream=stream: [f(eng) for f in stream])


def emit_ln_bn(k, stats, bnb, src_key, src_ap, sx=""):
    m = stats[:, 0:1]; rstd = stats[:, 1:2]; tmp = stats[:, 5:6]
    for c in range(8):
        k.op("v", lambda v, c=c: v.bn_stats(bnb[:, c * 6:(c + 1) * 6],
                                            src_ap[:, c * 512:(c + 1) * 512]),
             reads=(src_key,), writes=("bnb" + sx,))
    k.op("v", lambda v: v.bn_aggr(stats[:, 0:2], bnb[:]), reads=("bnb" + sx,),
         writes=("stm" + sx, "stvar" + sx))
    k.op("s", lambda s: s.activation(out=tmp, in_=rstd, func=AF.Ln, bias=EPS),
         reads=("stvar" + sx, "cst"), writes=("sttmp" + sx,))
    k.op("s", lambda s: s.activation(out=rstd, in_=tmp, func=AF.Exp, scale=-0.5),
         reads=("sttmp" + sx, "cst"), writes=("strstd" + sx, "stvar" + sx))
    return m, rstd


def emit_rstd_from(k, stats, s1ap, s2ap, n):
    m = stats[:, 0:1]; rstd = stats[:, 1:2]
    msq = stats[:, 4:5]; var = stats[:, 5:6]
    k.op("v", lambda v: v.tensor_scalar_mul(m, s1ap, 1.0 / n), reads=("lnacc",), writes=("stm",))
    k.op("v", lambda v: v.tensor_mul(msq, m, m), reads=("stm",), writes=("stmsq",))
    k.op("v", lambda v: v.scalar_tensor_tensor(out=var, in0=s2ap, scalar=1.0 / n, in1=msq,
                                               op0=ALU.mult, op1=ALU.subtract),
         reads=("lnacc", "stmsq"), writes=("stvar",))
    k.op("s", lambda s: s.activation(out=var, in_=var, func=AF.Ln, bias=EPS),
         reads=("stvar", "cst"), writes=("stvar",))
    k.op("s", lambda s: s.activation(out=rstd, in_=var, func=AF.Exp, scale=-0.5),
         reads=("stvar", "cst"), writes=("strstd",))
    return m, rstd


def build_nc(NT=8):
    T = NT * 128
    RNC = 2 if NT >= 2 else 1
    NPASS = NT // RNC
    NRC = (T + 511) // 512
    HIDT = min(512, T)
    NHB = T // HIDT
    NTT = HIDT // 128
    PW = max(512, 512 * NRC)

    nc = bass.Bass("TRN2", target_bir_lowering=False)
    x_d = nc.dram_tensor("x", [T, L], F32, kind="ExternalInput")
    wg_d = nc.dram_tensor("wg", [ITI, 128, HT, 128], I8, kind="ExternalInput")
    wu_d = nc.dram_tensor("wu", [ITI, 128, HT, 128], I8, kind="ExternalInput")
    wd_d = nc.dram_tensor("wd", [ITI, 128, L], I8, kind="ExternalInput")
    sg_d = nc.dram_tensor("sg", [128, ITI], F32, kind="ExternalInput")
    su_d = nc.dram_tensor("su", [128, ITI], F32, kind="ExternalInput")
    sdb_d = nc.dram_tensor("sdb", [128, L], F32, kind="ExternalInput")
    cos_d = nc.dram_tensor("cos_t", [KC, HT, 128, 128], BF16, kind="ExternalInput")
    sin_d = nc.dram_tensor("sin_t", [KC, HT, 128, 128], BF16, kind="ExternalInput")
    icos_d = nc.dram_tensor("icos_t", [KC, 128, L], BF16, kind="ExternalInput")
    id_d = nc.dram_tensor("ident", [128, 128], BF16, kind="ExternalInput")
    y_d = nc.dram_tensor("y", [T, L], F32, kind="ExternalOutput")

    a2_d = nc.dram_tensor("a2_scr", [T, 2 * L], F32, kind="Internal")
    a2b_d = nc.dram_tensor("a2b_scr", [T, 2 * L], BF16, kind="Internal")
    r_d = nc.dram_tensor("r_scr", [T, L], F32, kind="Internal")
    dn_d = nc.dram_tensor("dn_scr", [T, L], F32, kind="Internal")
    atf_d = nc.dram_tensor("atf_scr", [HT, 128, T], BF16, kind="Internal")
    hb_d = nc.dram_tensor("hb_scr", [ITI, 128, HIDT], BF16, kind="Internal") if NHB > 1 else None

    with ExitStack() as st:
        k = BK(nc, st)
        ident = st.enter_context(nc.sbuf_tensor("identb", [128, 128], BF16))
        stats2 = [st.enter_context(nc.sbuf_tensor(f"stats{q}", [128, 8], F32)) for q in range(2)]
        stats = stats2[0]
        bnb2 = [st.enter_context(nc.sbuf_tensor(f"bnb{q}", [128, 6 * 8], F32)) for q in range(2)]
        sg_sb = st.enter_context(nc.sbuf_tensor("sg_sb", [128, ITI], F32))
        su_sb = st.enter_context(nc.sbuf_tensor("su_sb", [128, ITI], F32))
        top8 = [st.enter_context(nc.sbuf_tensor(f"top8{q}", [128, 8], BF16)) for q in range(2)]
        t8f = [st.enter_context(nc.sbuf_tensor(f"t8f{q}", [128, 8], F32)) for q in range(2)]
        idx8 = [st.enter_context(nc.sbuf_tensor(f"idx8{q}", [128, 8], U32)) for q in range(2)]
        smb = [st.enter_context(nc.sbuf_tensor(f"smb{q}", [128, 8], F32)) for q in range(2)]
        w8 = [st.enter_context(nc.sbuf_tensor(f"w8b{q}", [128, 8], F32)) for q in range(2)]
        ssum = [st.enter_context(nc.sbuf_tensor(f"ssumb{q}", [128, 2], F32)) for q in range(2)]
        iot = [st.enter_context(nc.sbuf_tensor(f"iotb{q}", [128, 8], U32)) for q in range(2)]
        gidx = [st.enter_context(nc.sbuf_tensor(f"gidxb{q}", [128, 8], U32)) for q in range(2)]
        lnacc = st.enter_context(nc.sbuf_tensor("lnacc", [128, 3 * NT], F32))
        cst = st.enter_context(nc.sbuf_tensor("cstb", [128, 4], F32))
        for ci, cv in enumerate((EPS, -0.5, 1.0, 0.0)):
            nc.const_aps.register(F32, cv, cst[:, ci:ci + 1])
        k.op("v", lambda v: v.memset(cst[:, 0:1], EPS), writes=("cst",))
        k.op("v", lambda v: v.memset(cst[:, 1:2], -0.5), reads=("cst",), writes=("cst",))
        k.op("v", lambda v: v.memset(cst[:, 2:3], 1.0), reads=("cst",), writes=("cst",))
        k.op("v", lambda v: v.memset(cst[:, 3:4], 0.0), reads=("cst",), writes=("cst",))
        k.barrier()

        k.op("y", lambda y: y.dma_start(out=ident[:], in_=id_d[:]), writes=("ident",), dma=True)
        k.op("y", lambda y: y.dma_start(out=sg_sb[:], in_=sg_d[:]), writes=("sg",), dma=True)
        k.op("y", lambda y: y.dma_start(out=su_sb[:], in_=su_d[:]), writes=("su",), dma=True)

        s_aT = ExitStack()
        aT = s_aT.enter_context(nc.sbuf_tensor("aT", [128, HT, T], BF16))
        s_pT = ExitStack()
        pT = s_pT.enter_context(nc.sbuf_tensor("pT", [128, KC, T], BF16, side="right"))
        s_a1 = ExitStack()
        xs = [s_a1.enter_context(nc.sbuf_tensor(f"xs{p}", [128, L], F32)) for p in range(2)]
        abf = [s_a1.enter_context(nc.sbuf_tensor(f"abf{p}", [128, L], BF16)) for p in range(2)]
        sq = s_a1.enter_context(nc.sbuf_tensor("sq", [128, 2048], F32))
        cosb = [s_a1.enter_context(nc.sbuf_tensor(f"cosb{p}", [128, HT, 128], BF16)) for p in range(2)]
        sinb = [s_a1.enter_context(nc.sbuf_tensor(f"sinb{p}", [128, HT, 128], BF16)) for p in range(2)]
        tp_ps = [s_a1.enter_context(nc.psum_tensor(f"tp{p}", [128, 1024], BF16)) for p in range(2)]
        re_ps = [s_a1.enter_context(nc.psum_tensor(f"re{p}", [128, 512], F32)) for p in range(2)]
        im_ps = [s_a1.enter_context(nc.psum_tensor(f"im{p}", [128, 512], F32)) for p in range(2)]
        HALF = (NT + 1) // 2

        def emit_a1(i):
            p = i % 2
            rows = slice(i * 128, (i + 1) * 128)
            k.op("y", lambda y, rows=rows, p=p: y.dma_start(out=xs[p][:], in_=x_d[rows, :]),
                 writes=(f"xs{p}",), dma=True)
            m, rstd = emit_ln_bn(k, stats2[p], bnb2[p], f"xs{p}", xs[p][:], sx=str(p))
            k.op("v", lambda v, p=p, m=m, rstd=rstd: v.tensor_scalar(
                    out=xs[p][:], in0=xs[p][:], scalar1=m, scalar2=rstd,
                    op0=ALU.subtract, op1=ALU.mult),
                 reads=(f"xs{p}", f"stm{p}", f"strstd{p}"), writes=(f"xs{p}",))
            k.op("g", lambda g, rows=rows, p=p: g.dma_start(out=a2_d[rows, 0:L], in_=xs[p][:]),
                 reads=(f"xs{p}",), writes=("a2w",), dma=True)
            k.op("g", lambda g, rows=rows, p=p: g.dma_start(out=a2_d[rows, L:2 * L], in_=xs[p][:]),
                 reads=(f"xs{p}",), writes=("a2w",), dma=True)
            k.op("v", lambda v, p=p: v.tensor_copy(out=abf[p][:], in_=xs[p][:]),
                 reads=(f"xs{p}",), writes=(f"abf{p}",))
            k.op("g", lambda g, rows=rows, p=p: g.dma_start(out=a2b_d[rows, 0:L], in_=abf[p][:]),
                 reads=(f"abf{p}",), writes=("a2bw",), dma=True)
            k.op("g", lambda g, rows=rows, p=p: g.dma_start(out=a2b_d[rows, L:2 * L], in_=abf[p][:]),
                 reads=(f"abf{p}",), writes=("a2bw",), dma=True)

        def emit_tp(i, akey):
            p = i % 2
            for tc in range(HT):
                pp = tc % 2
                k.op("t", lambda t, tc=tc, pp=pp, p=p: t.transpose(
                        tp_ps[pp][:, 0:128], abf[p][:, tc * 128:(tc + 1) * 128], ident[:]),
                     reads=(f"abf{p}", "ident"), writes=(f"tp{pp}",))
                k.op("s", lambda sc, tc=tc, pp=pp, i=i: sc.copy(
                        out=aT[:, tc, i * 128:(i + 1) * 128], in_=tp_ps[pp][:, 0:128]),
                     reads=(f"tp{pp}",), writes=(akey,))

        def load_tab(kc):
            p = kc % 2
            k.op("s", lambda sc, kc=kc, p=p: sc.dma_start(
                    out=cosb[p][:], in_=cos_d[kc].rearrange("a p c -> p a c")),
                 writes=(f"cosb{p}",), dma=True)
            k.op("s", lambda sc, kc=kc, p=p: sc.dma_start(
                    out=sinb[p][:], in_=sin_d[kc].rearrange("a p c -> p a c")),
                 writes=(f"sinb{p}",), dma=True)

        def emit_fwd(rc, akey):
            w = min(512, T - rc * 512)
            if rc > 0:
                load_tab(0)
            for kc in range(KC):
                p = kc % 2
                if kc + 1 < KC:
                    load_tab(kc + 1)

                def fwd(t, p=p, rc=rc, w=w):
                    last = None
                    for tc in range(HT):
                        st_ = (tc == 0)
                        sp = (tc == HT - 1)
                        t.matmul(re_ps[p][:, 0:w], cosb[p][:, tc, :],
                                 aT[:, tc, rc * 512:rc * 512 + w], start=st_, stop=sp)
                        last = t.matmul(im_ps[p][:, 0:w], sinb[p][:, tc, :],
                                        aT[:, tc, rc * 512:rc * 512 + w], start=st_, stop=sp)
                    return last
                k.op("t", fwd, reads=(f"cosb{p}", f"sinb{p}", akey), writes=(f"re{p}", f"im{p}"))
                k.op("s", lambda sc, p=p, w=w: sc.activation(
                        out=sq[:, 0:w], in_=re_ps[p][:, 0:w], func=AF.Square),
                     reads=(f"re{p}", "cst"), writes=("sqa",))
                k.op("s", lambda sc, p=p, w=w: sc.activation(
                        out=sq[:, 1024:1024 + w], in_=im_ps[p][:, 0:w], func=AF.Square),
                     reads=(f"im{p}", "cst"), writes=("sqb",))
                k.op("v", lambda v, kc=kc, rc=rc, w=w: v.tensor_add(
                        pT[:, kc, rc * 512:rc * 512 + w], sq[:, 0:w], sq[:, 1024:1024 + w]),
                     reads=("sqa", "sqb"), writes=("pT",))

        load_tab(0)
        for i in range(NT):
            emit_a1(i)
            emit_tp(i, "aTa" if i < HALF else "aTb")
        emit_fwd(0, "aTa")
        if NT > 4:
            emit_fwd(1, "aTb")
        k.barrier()
        s_a1.close()
        s_aT.close()

        s_corr = ExitStack()
        corrF = s_corr.enter_context(nc.sbuf_tensor("corrF", [128, NT, L], BF16))
        s_att = ExitStack()
        HL = L // 4
        icosb = [s_att.enter_context(nc.sbuf_tensor(f"icosb{p}", [128, HL], BF16)) for p in range(3)]
        rollf = s_att.enter_context(nc.sbuf_tensor("rollf", [128, L], F32))
        rollb = ([s_att.enter_context(nc.sbuf_tensor(f"rollb{p}", [128, L], BF16))
                  for p in range(3)] if NROLL > 1 else [])
        da = [s_att.enter_context(nc.sbuf_tensor(f"da{q}", [128, L], F32)) for q in range(2)]
        stg = [s_att.enter_context(nc.sbuf_tensor(f"stg{p}", [128, HT, 128], BF16)) for p in range(2)]
        s_attp = ExitStack()
        cr_ps = [s_attp.enter_context(nc.psum_tensor(f"cr{r}", [128, HL], F32))
                 for r in range(2 * RNC)]

        def load_icos(n, seq):
            ps_, th, kc = seq[n]
            p = n % 3
            k.op("y", lambda y, kc=kc, th=th, p=p: y.dma_start(
                    out=icosb[p][:], in_=icos_d[kc, :, th * HL:(th + 1) * HL]),
                 writes=(f"icosb{p}",), dma=True)

        icos_seq = [(ps_, th, kc) for ps_ in range(NPASS) for th in range(4) for kc in range(KC)]
        load_icos(0, icos_seq)
        load_icos(1, icos_seq)
        nld = 2
        pend = []

        def emit_a456(i):
            q = i % 2
            sx = str(q)
            rows = slice(i * 128, (i + 1) * 128)
            k.op("v", lambda v, i=i, q=q: v.max(out=top8[q][:], in_=corrF[:, i, :]),
                 reads=(f"corr{i}",), writes=("top8" + sx,))
            k.op("v", lambda v, i=i, q=q: v.max_index(out=idx8[q][:], in_max=top8[q][:],
                                                 in_values=corrF[:, i, :]),
                 reads=("top8" + sx, f"corr{i}"), writes=("idx8" + sx,))
            k.op("v", lambda v, q=q: v.tensor_copy(out=t8f[q][:], in_=top8[q][:]),
                 reads=("top8" + sx,), writes=("t8f" + sx,))
            k.op("v", lambda v, q=q: v.tensor_scalar(out=smb[q][:], in0=t8f[q][:],
                                                scalar1=t8f[q][:, 0:1],
                                                scalar2=None, op0=ALU.subtract),
                 reads=("t8f" + sx,), writes=("smbb" + sx,))
            k.op("s", lambda s, q=q: s.activation(out=smb[q][:], in_=smb[q][:], func=AF.Exp),
                 reads=("smbb" + sx, "cst"), writes=("smbb" + sx,))
            k.op("v", lambda v, q=q: v.reduce_sum(ssum[q][:, 0:1], smb[q][:], axis=AX),
                 reads=("smbb" + sx,), writes=("ssum" + sx,))
            k.op("v", lambda v, q=q: v.reciprocal(ssum[q][:, 1:2], ssum[q][:, 0:1]),
                 reads=("ssum" + sx,), writes=("ssumr" + sx,))
            k.op("v", lambda v, q=q: v.tensor_scalar(out=w8[q][:], in0=smb[q][:],
                                                scalar1=ssum[q][:, 1:2],
                                                scalar2=None, op0=ALU.mult),
                 reads=("smbb" + sx, "ssumr" + sx), writes=("w8" + sx,))
            k.op("g", lambda g, i=i, q=q: g.iota(iot[q][:], pattern=[[0, 8]],
                                            base=i * 128 * 2 * L,
                                            channel_multiplier=2 * L),
                 writes=("iot" + sx,))
            k.op("v", lambda v, q=q: v.tensor_tensor(out=gidx[q][:], in0=idx8[q][:],
                                                in1=iot[q][:], op=ALU.add),
                 reads=("idx8" + sx, "iot" + sx), writes=("gidx" + sx,))
            k.op("g", lambda g, q=q: g.indirect_dma_start(
                    out=rollf[:], out_offset=None, in_=a2_d[:, :],
                    in_offset=bass.IndirectOffsetOnAxis(ap=gidx[q][:, 0:1], axis=1)),
                 reads=("gidx" + sx,), writes=("rollf",), dma=True)
            k.op("v", lambda v, q=q: v.tensor_scalar(
                    out=da[q][:], in0=rollf[:], scalar1=w8[q][:, 0:1], scalar2=None,
                    op0=ALU.mult),
                 reads=("rollf", "w8" + sx), writes=("da" + sx,))
            for j in range(1, NROLL):
                p = (i * (TOPK - 1) + j) % 3
                k.op("g", lambda g, j=j, p=p, q=q: g.indirect_dma_start(
                        out=rollb[p][:], out_offset=None, in_=a2b_d[:, :],
                        in_offset=bass.IndirectOffsetOnAxis(ap=gidx[q][:, j:j + 1], axis=1)),
                     reads=("gidx" + sx,), writes=(f"rollb{p}",), dma=True)
                k.op("v", lambda v, j=j, p=p, q=q: v.scalar_tensor_tensor(
                        out=da[q][:], in0=rollb[p][:], scalar=w8[q][:, j:j + 1], in1=da[q][:],
                        op0=ALU.mult, op1=ALU.add),
                     reads=(f"rollb{p}", "w8" + sx, "da" + sx), writes=("da" + sx,))
            m, rstd = emit_ln_bn(k, stats2[q], bnb2[q], "da" + sx, da[q][:], sx=sx)
            k.op("v", lambda v, q=q, m=m, rstd=rstd: v.tensor_scalar(
                    out=da[q][:], in0=da[q][:], scalar1=m, scalar2=rstd,
                    op0=ALU.subtract, op1=ALU.mult),
                 reads=("da" + sx, "stm" + sx, "strstd" + sx), writes=("da" + sx,))

        def emit_a456_B(i):
            q = i % 2
            sx = str(q)
            rows = slice(i * 128, (i + 1) * 128)
            k.op("g", lambda g, rows=rows, q=q: g.dma_start(
                    out=da[q][:], in_=x_d[rows, :], accum_op=ALU.add,
                    max_dma_last_dim=2048),
                 reads=("da" + sx,), writes=("da" + sx,), dma=True)
            k.op("g", lambda g, rows=rows, q=q: g.dma_start(out=r_d[rows, :], in_=da[q][:]),
                 reads=("da" + sx,), writes=("rw",), dma=True)
            m, rstd = emit_ln_bn(k, stats2[q], bnb2[q], "da" + sx, da[q][:], sx=sx)
            k.op("v", lambda v, i=i, q=q, m=m, rstd=rstd: v.tensor_scalar(
                    out=corrF[:, i, :], in0=da[q][:], scalar1=m, scalar2=rstd,
                    op0=ALU.subtract, op1=ALU.mult),
                 reads=("da" + sx, "stm" + sx, "strstd" + sx), writes=(f"corr{i}",))
        for ps_ in range(NPASS):
            for th in range(4):
                for kc in range(KC):
                    if nld < len(icos_seq):
                        load_icos(nld, icos_seq)
                        nld += 1
                    p = (ps_ * 4 * KC + th * KC + kc) % 3
                    cpar = ((ps_ * 4 + th) % 2) * RNC

                    def corr(t, ps_=ps_, kc=kc, p=p, cpar=cpar):
                        last = None
                        for rt in range(RNC):
                            row = (ps_ * RNC + rt) * 128
                            for tch in range(HL // 512):
                                last = t.matmul(cr_ps[cpar + rt][:, tch * 512:(tch + 1) * 512],
                                                pT[:, kc, row:row + 128],
                                                icosb[p][:, tch * 512:(tch + 1) * 512],
                                                start=(kc == 0), stop=(kc == KC - 1))
                        return last
                    k.op("t", corr, reads=(f"icosb{p}", "pT"),
                         writes=tuple(f"cr{cpar + rt}" for rt in range(RNC)))
                cpar = ((ps_ * 4 + th) % 2) * RNC
                for rt in range(RNC):
                    i = ps_ * RNC + rt
                    k.op("s", lambda sc, i=i, rt=rt, th=th, cpar=cpar: sc.copy(
                            out=corrF[:, i, th * HL:(th + 1) * HL], in_=cr_ps[cpar + rt][:]),
                         reads=(f"cr{cpar + rt}",), writes=(f"corr{i}",))
            for rt in range(RNC):
                i = ps_ * RNC + rt
                emit_a456(i)
                if pend:
                    emit_a456_B(pend.pop(0))
                pend.append(i)
        while pend:
            emit_a456_B(pend.pop(0))
        s_attp.close()
        s_tq = ExitStack()
        tq_ps = [s_tq.enter_context(nc.psum_tensor(f"tq{p}", [128, 1024], BF16)) for p in range(2)]
        for i in range(NT):
            sp2 = i % 2
            for tc in range(HT):
                pp = tc % 2
                k.op("t", lambda t, i=i, tc=tc, pp=pp: t.transpose(
                        tq_ps[pp][:, 0:128], corrF[:, i, tc * 128:(tc + 1) * 128], ident[:]),
                     reads=(f"corr{i}", "ident"), writes=(f"tq{pp}",))
                k.op("s", lambda s, tc=tc, pp=pp, sp2=sp2: s.copy(
                        out=stg[sp2][:, tc, :], in_=tq_ps[pp][:, 0:128]),
                     reads=(f"tq{pp}",), writes=(f"stg{sp2}",))
            for hf in range(2):
                k.op("y", lambda y, i=i, hf=hf, sp2=sp2: y.dma_start(
                        out=atf_d[hf * 16:(hf + 1) * 16, :, i * 128:(i + 1) * 128]
                            .rearrange("a p c -> p a c"),
                        in_=stg[sp2][:, hf * 16:(hf + 1) * 16, :]),
                     reads=(f"stg{sp2}",), writes=("atfw",), dma=True)
        k.barrier()
        s_tq.close()
        s_att.close()
        s_corr.close()
        s_pT.close()

        ffn = ExitStack()
        hid = ffn.enter_context(nc.sbuf_tensor("hid", [128, ITI, HIDT], BF16))
        s_gu = ExitStack()
        atfb = s_gu.enter_context(nc.sbuf_tensor("atfb", [128, HT, T], BF16))
        wgb = [s_gu.enter_context(nc.sbuf_tensor(f"wgb{h}", [128, 16, 128], BF16)) for h in range(2)]
        wub = [s_gu.enter_context(nc.sbuf_tensor(f"wub{h}", [128, 16, 128], BF16)) for h in range(2)]
        silu_s = [s_gu.enter_context(nc.sbuf_tensor(f"silu{p}", [128, T], F32)) for p in range(2)]
        hstg = [s_gu.enter_context(nc.sbuf_tensor(f"hstg{p}", [128, HIDT], BF16)) for p in range(2)]
        gps = [s_gu.enter_context(nc.psum_tensor(f"gps{q}", [128, PW], F32)) for q in range(4)]

        for hf in range(2):
            k.op("y", lambda y, hf=hf: y.dma_start(
                    out=atfb[:, hf * 16:(hf + 1) * 16, :],
                    in_=atf_d[hf * 16:(hf + 1) * 16, :, :].rearrange("a p c -> p a c")),
                 writes=("atfb",), dma=True)

        for ig in range(ITI):
            par = ig % 2
            pg = gps[par * 2 + 0]
            pu = gps[par * 2 + 1]
            pgk = f"gps{par * 2}"
            puk = f"gps{par * 2 + 1}"
            for h in range(2):
                k.op("g", lambda g, ig=ig, h=h: g.dma_start(
                        out=wgb[h][:], in_=wg_d[ig, :, h * 16:(h + 1) * 16, :]),
                     writes=(f"wgb{h}",), dma=True)
                k.op("g", lambda g, ig=ig, h=h: g.dma_start(
                        out=wub[h][:], in_=wu_d[ig, :, h * 16:(h + 1) * 16, :]),
                     writes=(f"wub{h}",), dma=True)
            for h in range(2):
                def gu(t, h=h, pg=pg, pu=pu):
                    last = None
                    for hh in range(16):
                        ht = h * 16 + hh
                        st_ = ht == 0
                        sp = ht == HT - 1
                        for rc in range(NRC):
                            w = min(512, T - rc * 512)
                            rhs = atfb[:, ht, rc * 512:rc * 512 + w]
                            t.matmul(pg[:, rc * 512:rc * 512 + w], wgb[h][:, hh, :],
                                     rhs, start=st_, stop=sp)
                            last = t.matmul(pu[:, rc * 512:rc * 512 + w], wub[h][:, hh, :],
                                            rhs, start=st_, stop=sp)
                    return last
                k.op("t", gu, reads=(f"wgb{h}", f"wub{h}", "atfb"), writes=(pgk, puk))
            sp_ = ig % 2
            k.op("s", lambda s, ig=ig, pg=pg, sp_=sp_: s.activation(
                    out=silu_s[sp_][:], in_=pg[:, 0:T], func=AF.Silu,
                    scale=sg_sb[:, ig:ig + 1]),
                 reads=(pgk, "sg", "cst"), writes=(f"silu{sp_}",))
            k.op("v", lambda v, ig=ig, pu=pu, sp_=sp_: v.scalar_tensor_tensor(
                    out=hid[:, ig, :], in0=pu[:, 0:HIDT], scalar=su_sb[:, ig:ig + 1],
                    in1=silu_s[sp_][:, 0:HIDT], op0=ALU.mult, op1=ALU.mult),
                 reads=(puk, "su", f"silu{sp_}"), writes=(f"hid{ig % 4}",))
            if NHB > 1:
                k.op("v", lambda v, ig=ig, pu=pu, sp_=sp_: v.scalar_tensor_tensor(
                        out=hstg[sp_][:], in0=pu[:, HIDT:T], scalar=su_sb[:, ig:ig + 1],
                        in1=silu_s[sp_][:, HIDT:T], op0=ALU.mult, op1=ALU.mult),
                     reads=(puk, "su", f"silu{sp_}"), writes=(f"hstg{sp_}",))
                k.op("y", lambda y, ig=ig, sp_=sp_: y.dma_start(
                        out=hb_d[ig], in_=hstg[sp_][:]),
                     reads=(f"hstg{sp_}",), writes=("hbw",), dma=True)
        k.barrier()
        s_gu.close()

        s_dn = ExitStack()
        sdb = s_dn.enter_context(nc.sbuf_tensor("sdb_sb", [128, L], F32))
        wdb = [s_dn.enter_context(nc.sbuf_tensor(f"wdb{p}", [128, 8, 512], BF16)) for p in range(3)]
        dstage = [s_dn.enter_context(nc.sbuf_tensor(f"dst{p}", [128, 512], F32)) for p in range(2)]
        sqd = s_dn.enter_context(nc.sbuf_tensor("sqd", [128, 512], F32))
        dnc = [s_dn.enter_context(nc.sbuf_tensor(f"dnc{p}", [128, 512], F32)) for p in range(2)]
        rc_ = [s_dn.enter_context(nc.sbuf_tensor(f"rc{p}", [128, 512], F32)) for p in range(2)]
        dps = [s_dn.enter_context(nc.psum_tensor(f"dps{q}", [128, 512], F32)) for q in range(8)]
        k.op("y", lambda y: y.dma_start(out=sdb[:], in_=sdb_d[:]), writes=("sdb",), dma=True)

        def emit_ln4(tta):
            rows = slice(tta * 128, (tta + 1) * 128)
            m, rstd = emit_rstd_from(k, stats, lnacc[:, tta:tta + 1],
                                     lnacc[:, NT + tta:NT + tta + 1], L)
            for hc in range(8):
                cp = hc % 2
                cols = slice(hc * 512, (hc + 1) * 512)
                k.op("s", lambda s, rows=rows, cols=cols, cp=cp: s.dma_start(
                        out=dnc[cp][:], in_=dn_d[rows, cols]),
                     reads=("dnw",), writes=(f"dnc{cp}",), dma=True)
                k.op("s", lambda s, rows=rows, cols=cols, cp=cp: s.dma_start(
                        out=rc_[cp][:], in_=r_d[rows, cols]),
                     writes=(f"rc{cp}",), dma=True)
                k.op("v", lambda v, cp=cp, m=m, rstd=rstd: v.tensor_scalar(
                        out=dnc[cp][:], in0=dnc[cp][:], scalar1=m, scalar2=rstd,
                        op0=ALU.subtract, op1=ALU.mult),
                     reads=(f"dnc{cp}", "stm", "strstd"), writes=(f"dnc{cp}",))
                k.op("v", lambda v, cp=cp: v.tensor_add(dnc[cp][:], dnc[cp][:], rc_[cp][:]),
                     reads=(f"dnc{cp}", f"rc{cp}"), writes=(f"dnc{cp}",))
                k.op("y", lambda y, rows=rows, cols=cols, cp=cp: y.dma_start(
                        out=y_d[rows, cols], in_=dnc[cp][:]),
                     reads=(f"dnc{cp}",), writes=("yw",), dma=True)

        wdq = [s_dn.enter_context(nc.sbuf_tensor(f"wdq{p}", [128, 4, 2048], BF16))
               for p in range(2)] if NHB > 1 else []

        def emit_down_last(half):
            for tp_ in range(NTT // 2):
                for hh in range(2):
                    pdk = tuple(f"dps{q}" for q in range(8))
                    for io4 in range(22):
                        cnt = min(4, ITI - io4 * 4)
                        wp = io4 % 2
                        k.op("g", lambda g, hh=hh, io4=io4, cnt=cnt, wp=wp: g.dma_start(
                                out=wdq[wp][:, 0:cnt, :],
                                in_=wd_d[io4 * 4:io4 * 4 + cnt, :,
                                         hh * 2048:(hh + 1) * 2048]
                                    .rearrange("a p c -> p a c")),
                             writes=(f"wdq{wp}",), dma=True)

                        def dnq(t, io4=io4, cnt=cnt, wp=wp, tp_=tp_):
                            last = None
                            for ii in range(cnt):
                                i = io4 * 4 + ii
                                for tt2 in range(2):
                                    for hq in range(4):
                                        last = t.matmul(
                                            dps[tt2 * 4 + hq][:, 0:512],
                                            hid[:, i, (tp_ * 2 + tt2) * 128:
                                                      (tp_ * 2 + tt2 + 1) * 128],
                                            wdq[wp][:, ii, hq * 512:(hq + 1) * 512],
                                            start=(i == 0), stop=(i == ITI - 1))
                            return last
                        k.op("t", dnq, reads=(f"wdq{wp}",) + tuple(f"hid{q}" for q in range(4)),
                             writes=pdk)
                    for tt2 in range(2):
                        tta = half * NTT + tp_ * 2 + tt2
                        for hq in range(4):
                            hc = hh * 4 + hq
                            dp = (tt2 * 4 + hq) % 2
                            k.op("v", lambda v, hc=hc, tt2=tt2, hq=hq, dp=dp: v.tensor_tensor(
                                    out=dstage[dp][:], in0=dps[tt2 * 4 + hq][:, 0:512],
                                    in1=sdb[:, hc * 512:(hc + 1) * 512], op=ALU.mult),
                                 reads=(f"dps{tt2 * 4 + hq}", "sdb"), writes=(f"dst{dp}",))
                            k.op("y", lambda y, tta=tta, hc=hc, dp=dp: y.dma_start(
                                    out=dn_d[tta * 128:(tta + 1) * 128,
                                             hc * 512:(hc + 1) * 512],
                                    in_=dstage[dp][:]),
                                 reads=(f"dst{dp}",), writes=("dnw",), dma=True)
                            k.op("v", lambda v, tta=tta, dp=dp: v.reduce_sum(
                                    lnacc[:, 2 * NT + tta:2 * NT + tta + 1],
                                    dstage[dp][:], axis=AX),
                                 reads=(f"dst{dp}",), writes=("lntmp",))
                            first = hh == 0 and hq == 0
                            if first:
                                k.op("v", lambda v, tta=tta: v.tensor_copy(
                                        out=lnacc[:, tta:tta + 1],
                                        in_=lnacc[:, 2 * NT + tta:2 * NT + tta + 1]),
                                     reads=("lntmp",), writes=("lnacc",))
                            else:
                                k.op("v", lambda v, tta=tta: v.tensor_add(
                                        lnacc[:, tta:tta + 1], lnacc[:, tta:tta + 1],
                                        lnacc[:, 2 * NT + tta:2 * NT + tta + 1]),
                                     reads=("lntmp", "lnacc"), writes=("lnacc",))
                            k.op("s", lambda s, dp=dp: s.activation(
                                    out=sqd[:], in_=dstage[dp][:], func=AF.Square,
                                    accum_out=stats[:, 6:7]),
                                 reads=(f"dst{dp}", "cst"), writes=("lntmp2",))
                            if first:
                                k.op("v", lambda v, tta=tta: v.tensor_copy(
                                        out=lnacc[:, NT + tta:NT + tta + 1],
                                        in_=stats[:, 6:7]),
                                     reads=("lntmp2",), writes=("lnacc",))
                            else:
                                k.op("v", lambda v, tta=tta: v.tensor_add(
                                        out=lnacc[:, NT + tta:NT + tta + 1],
                                        in0=lnacc[:, NT + tta:NT + tta + 1],
                                        in1=stats[:, 6:7]),
                                     reads=("lntmp2", "lnacc"), writes=("lnacc",))
                for tt2 in range(2):
                    emit_ln4(half * NTT + tp_ * 2 + tt2)

        for half in range(NHB):
            if NHB > 1 and half == NHB - 1:
                emit_down_last(half)
                continue
            for hc in range(8):
                par = hc % 2
                pd = [dps[par * 4 + q] for q in range(NTT)]
                pdkeys = tuple(f"dps{par * 4 + q}" for q in range(NTT))
                for io in range(11):
                    cnt = min(8, ITI - io * 8)
                    wp = (hc * 11 + io) % 3
                    k.op("g", lambda g, hc=hc, io=io, cnt=cnt, wp=wp: g.dma_start(
                            out=wdb[wp][:, 0:cnt, :],
                            in_=wd_d[io * 8:io * 8 + cnt, :, hc * 512:(hc + 1) * 512]
                                .rearrange("a p c -> p a c")),
                         writes=(f"wdb{wp}",), dma=True)

                    def dn(t, io=io, cnt=cnt, wp=wp, pd=pd):
                        last = None
                        for ii in range(cnt):
                            i = io * 8 + ii
                            for tt in range(NTT):
                                last = t.matmul(pd[tt][:, 0:512],
                                                hid[:, i, tt * 128:(tt + 1) * 128],
                                                wdb[wp][:, ii, :],
                                                start=(i == 0), stop=(i == ITI - 1))
                        return last
                    k.op("t", dn, reads=(f"wdb{wp}",) + tuple(f"hid{q}" for q in range(4)),
                         writes=pdkeys)
                for tt in range(NTT):
                    tta = half * NTT + tt
                    dp = tt % 2
                    k.op("v", lambda v, hc=hc, tt=tt, dp=dp, pd=pd: v.tensor_tensor(
                            out=dstage[dp][:], in0=pd[tt][:, 0:512],
                            in1=sdb[:, hc * 512:(hc + 1) * 512], op=ALU.mult),
                         reads=(pdkeys[tt], "sdb"), writes=(f"dst{dp}",))
                    k.op("y", lambda y, tta=tta, hc=hc, dp=dp: y.dma_start(
                            out=dn_d[tta * 128:(tta + 1) * 128, hc * 512:(hc + 1) * 512],
                            in_=dstage[dp][:]),
                         reads=(f"dst{dp}",), writes=("dnw",), dma=True)
                    k.op("v", lambda v, tta=tta, dp=dp: v.reduce_sum(
                            lnacc[:, 2 * NT + tta:2 * NT + tta + 1], dstage[dp][:], axis=AX),
                         reads=(f"dst{dp}",), writes=("lntmp",))
                    if hc == 0:
                        k.op("v", lambda v, tta=tta: v.tensor_copy(
                                out=lnacc[:, tta:tta + 1],
                                in_=lnacc[:, 2 * NT + tta:2 * NT + tta + 1]),
                             reads=("lntmp",), writes=("lnacc",))
                    else:
                        k.op("v", lambda v, tta=tta: v.tensor_add(
                                lnacc[:, tta:tta + 1], lnacc[:, tta:tta + 1],
                                lnacc[:, 2 * NT + tta:2 * NT + tta + 1]),
                             reads=("lntmp", "lnacc"), writes=("lnacc",))
                    k.op("s", lambda s, dp=dp: s.activation(
                            out=sqd[:], in_=dstage[dp][:], func=AF.Square,
                            accum_out=stats[:, 6:7]),
                         reads=(f"dst{dp}", "cst"), writes=("lntmp2",))
                    if hc == 0:
                        k.op("v", lambda v, tta=tta: v.tensor_copy(
                                out=lnacc[:, NT + tta:NT + tta + 1], in_=stats[:, 6:7]),
                             reads=("lntmp2",), writes=("lnacc",))
                    else:
                        k.op("v", lambda v, tta=tta: v.tensor_add(
                                lnacc[:, NT + tta:NT + tta + 1],
                                lnacc[:, NT + tta:NT + tta + 1], stats[:, 6:7]),
                             reads=("lntmp2", "lnacc"), writes=("lnacc",))
            if half + 1 < NHB:
                for ch in range(8):
                    i0 = (ITI * ch) // 8
                    i1 = (ITI * (ch + 1)) // 8
                    k.op("y", lambda y, i0=i0, i1=i1: y.dma_start(
                            out=hid[:, i0:i1, :],
                            in_=hb_d[i0:i1, :, :].rearrange("a p c -> p a c")),
                         reads=("hbw",) + tuple(f"hid{q}" for q in range(4)),
                         writes=tuple(f"hid{q}" for q in range(4)), dma=True)
            for tt in range(NTT):
                emit_ln4(half * NTT + tt)

        k.barrier()
        s_dn.close()
        ffn.close()
        with nc.Block() as block:
            k.replay(block)

    nc.finalize()
    return nc


_NC_CACHE = {}


def _get_nc(NT=8):
    if NT not in _NC_CACHE:
        _NC_CACHE[NT] = build_nc(NT)
    return _NC_CACHE[NT]


def prep_weights(w_gate, w_up, w_down, s_gate, s_up, s_down):
    wg = np.ascontiguousarray(np.asarray(w_gate, dtype=np.int8)
                              .reshape(ITI, 128, HT, 128).transpose(0, 3, 2, 1))
    wu = np.ascontiguousarray(np.asarray(w_up, dtype=np.int8)
                              .reshape(ITI, 128, HT, 128).transpose(0, 3, 2, 1))
    wd = np.ascontiguousarray(np.asarray(w_down, dtype=np.int8).T.reshape(ITI, 128, L))
    sg = np.ascontiguousarray(np.asarray(s_gate, dtype=np.float32).reshape(ITI, 128).T)
    su = np.ascontiguousarray(np.asarray(s_up, dtype=np.float32).reshape(ITI, 128).T)
    sdb = np.ascontiguousarray(
        np.broadcast_to(np.asarray(s_down, dtype=np.float32)[None, :], (128, L)))
    return wg, wu, wd, sg, su, sdb


def kernel(L0_attn_x, L0_ffn_w_gate, L0_ffn_s_gate, L0_ffn_w_up, L0_ffn_s_up,
           L0_ffn_w_down, L0_ffn_s_down):
    x = np.ascontiguousarray(np.asarray(L0_attn_x, dtype=np.float32)).reshape(2 * L, L)
    wg, wu, wd, sg, su, sdb = prep_weights(L0_ffn_w_gate, L0_ffn_w_up, L0_ffn_w_down,
                                           L0_ffn_s_gate, L0_ffn_s_up, L0_ffn_s_down)
    cos_t, sin_t, icos_t, ident = host_tables()
    nc = _get_nc(8)
    T = 1024
    in_maps = []
    for c in range(NCORES):
        in_maps.append(dict(
            x=np.ascontiguousarray(x[c * T:(c + 1) * T]),
            wg=wg, wu=wu, wd=wd, sg=sg, su=su, sdb=sdb,
            cos_t=cos_t, sin_t=sin_t, icos_t=icos_t, ident=ident,
        ))
    res = run_bass_kernel_spmd(nc, in_maps, core_ids=list(range(NCORES)))
    out = np.concatenate([res.results[c]["y"] for c in range(NCORES)], axis=0)
    return out.reshape(2, L, L)
